# version 08
# baseline (speedup 1.0000x reference)
import contextlib
import numpy as np
import ml_dtypes
import concourse.bass as bass
import concourse.mybir as mybir
from concourse.bass_utils import run_bass_kernel_spmd

F32 = mybir.dt.float32
BF16 = mybir.dt.bfloat16
I32 = mybir.dt.int32
AF = mybir.ActivationFunctionType
ALU = mybir.AluOpType
AX = mybir.AxisListType
EP = mybir.EngineType

B = 4
N = 8192
D = 1024
T = 4096
NT = 32
DT = 8
WIN = 512
HEADS = 8
DH = 64
INNER = 512
HID = 2730
HID2 = 5460
QKV = 1536
NEG = -3.0e38
EPS = 1e-5
NJT = 64
IG_SIZES = [3] * 10 + [2]
HT_SIZES = [128] * 21 + [42]
NHT = len(HT_SIZES)
HT_OFF = [0]
for _s in HT_SIZES[:-1]:
    HT_OFF.append(HT_OFF[-1] + _s)
TBL_ROWS = T + WIN
TBL_T = TBL_ROWS // 128


class S:
    def __init__(self, ctx, nc, name):
        self.h = ctx.enter_context(nc.semaphore(name))
        self.n = 0

    def inc(self, inst, k=1):
        inst.then_inc(self.h, k)
        self.n += k
        return inst


def build_nc(debug=False, stages=99, comm=True, race=True):
    nc = bass.Bass("TRN2", target_bir_lowering=False, detect_race_conditions=race)

    x_my = nc.dram_tensor("x_my", [T, D], F32, kind="ExternalInput").ap()
    x_oth = nc.dram_tensor("x_oth", [T, D], F32, kind="ExternalInput").ap()
    wqkv_in = nc.dram_tensor("wqkv", [2, D, QKV], F32, kind="ExternalInput").ap()
    wo_in = nc.dram_tensor("wo", [2, INNER, D], F32, kind="ExternalInput").ap()
    w1_in = nc.dram_tensor("w1", [2, D, HID2], F32, kind="ExternalInput").ap()
    w2_in = nc.dram_tensor("w2", [2, HID, D], F32, kind="ExternalInput").ap()
    qs_in = nc.dram_tensor("qscale", [2, DH], F32, kind="ExternalInput").ap()
    ks_in = nc.dram_tensor("kscale", [2, DH], F32, kind="ExternalInput").ap()
    lg_in = nc.dram_tensor("ln_g", [2, D], F32, kind="ExternalInput").ap()
    lb_in = nc.dram_tensor("ln_b", [2, D], F32, kind="ExternalInput").ap()
    b1_in = nc.dram_tensor("b1", [2, HID2], F32, kind="ExternalInput").ap()
    b2_in = nc.dram_tensor("b2", [2, D], F32, kind="ExternalInput").ap()
    tq_c = nc.dram_tensor("tqc", [128, TBL_T * DH], F32, kind="ExternalInput").ap()
    tq_s = nc.dram_tensor("tqs", [128, TBL_T * DH], F32, kind="ExternalInput").ap()
    tk_c = nc.dram_tensor("tkc", [128, TBL_T * DH], F32, kind="ExternalInput").ap()
    tk_s = nc.dram_tensor("tks", [128, TBL_T * DH], F32, kind="ExternalInput").ap()
    bigmask_in = nc.dram_tensor("bigmask", [128, 1408], BF16, kind="ExternalInput").ap()
    maskw0_in = nc.dram_tensor("maskw0", [8, 128, WIN], BF16, kind="ExternalInput").ap()
    idb_in = nc.dram_tensor("idb", [128, 128], BF16, kind="ExternalInput").ap()
    idf_in = nc.dram_tensor("idf", [128, 128], F32, kind="ExternalInput").ap()
    onesf_in = nc.dram_tensor("onesf", [1, 128], F32, kind="ExternalInput").ap()
    nonce_in = nc.dram_tensor("nonce", [1, 1], I32, kind="ExternalInput").ap()
    hs_in = nc.dram_tensor("hscale", [128, 1], F32, kind="ExternalInput").ap()

    out_my = nc.dram_tensor("out_my", [T, D], F32, kind="ExternalOutput").ap()

    lnbuf = nc.dram_tensor("lnbuf", [N, D], BF16).ap()
    sspill = nc.dram_tensor("sspill", [NJT, 8, 128, 512], BF16).ap()
    xmidbuf = nc.dram_tensor("xmidbuf", [T, D], F32).ap()
    y0buf = nc.dram_tensor("y0buf", [T + 513, D], F32).ap()
    y1buf = nc.dram_tensor("y1buf", [T + 1, D], F32).ap()
    y2buf = nc.dram_tensor("y2buf", [T + 513, D], F32).ap()
    y3buf = nc.dram_tensor("y3buf", [T + 1, D], F32).ap()
    y4buf = nc.dram_tensor("y4buf", [T, D], F32).ap()

    if debug:
        dbg = {}
        for nm, buf in [("dbg_xmid", xmidbuf), ("dbg_y0", y0buf), ("dbg_y1", y1buf),
                        ("dbg_y2", y2buf), ("dbg_y3", y3buf), ("dbg_y4", y4buf)]:
            dbg[nm] = (nc.dram_tensor(nm, list(buf.shape), F32, kind="ExternalOutput").ap(), buf)

    xch_stats = nc.dram_tensor("xch_stats", [8, 2, 128, NJT], F32, addr_space="Shared").ap()
    xch_xmid = nc.dram_tensor("xch_xmid", [8, 513, D], F32, addr_space="Shared").ap()
    xch_y1 = nc.dram_tensor("xch_y1", [8, 1, D], F32, addr_space="Shared").ap()
    xch_y2 = nc.dram_tensor("xch_y2", [8, 513, D], F32, addr_space="Shared").ap()
    xch_y3 = nc.dram_tensor("xch_y3", [8, 1, D], F32, addr_space="Shared").ap()
    xch_flag = nc.dram_tensor("xch_flag", [5, 8], I32, addr_space="Shared").ap()

    ctx = contextlib.ExitStack()
    with ctx:
        E = ctx.enter_context

        idb = E(nc.sbuf_tensor("s_idb", [128, 128], BF16))
        idf = E(nc.sbuf_tensor("s_idf", [128, 128], F32))
        onesf = E(nc.sbuf_tensor("s_onesf", [1, 128], F32))
        snonce = E(nc.sbuf_tensor("s_nonce", [1, 1], I32))
        sflag = E(nc.sbuf_tensor("s_flag", [1, 1], I32))
        shsc = E(nc.sbuf_tensor("s_hsc", [128, 1], F32))
        stM = E(nc.sbuf_tensor("s_stM", [128, NJT], F32))
        stZ = E(nc.sbuf_tensor("s_stZ", [128, NJT], F32))
        pM = E(nc.sbuf_tensor("s_pM", [128, NJT], F32))
        pZ = E(nc.sbuf_tensor("s_pZ", [128, NJT], F32))
        negMg = E(nc.sbuf_tensor("s_negMg", [128, NJT], F32))
        recipZ = E(nc.sbuf_tensor("s_recipZ", [128, NJT], F32))
        tmpw = E(nc.sbuf_tensor("s_tmpw", [128, NJT], F32))
        mu_t = E(nc.sbuf_tensor("s_mu", [128, 1], F32))
        tm_t = E(nc.sbuf_tensor("s_tm", [128, 1], F32))
        sa_t = E(nc.sbuf_tensor("s_sa", [128, 1], F32))
        sb_t = E(nc.sbuf_tensor("s_sb", [128, 1], F32))
        ve_t = E(nc.sbuf_tensor("s_ve", [128, 1], F32))
        rstd_t = E(nc.sbuf_tensor("s_rstd", [128, 1], F32))
        negb_t = E(nc.sbuf_tensor("s_negb", [128, 1], F32))
        sqd = E(nc.sbuf_tensor("s_sqd", [128, 1024], F32))
        halorow = E(nc.sbuf_tensor("s_halorow", [1, D], F32))
        halot = E(nc.sbuf_tensor("s_halot", [128, D], F32))

        Lq = S(ctx, nc, "Lq")
        Pq = S(ctx, nc, "Pq")
        LP = [S(ctx, nc, f"LP{i}") for i in range(2)]
        SR = [S(ctx, nc, f"SR{i}") for i in range(8)]
        SPq = [S(ctx, nc, f"SPq{i}") for i in range(2)]
        Tq = S(ctx, nc, "Tq")
        Tq2 = [S(ctx, nc, f"Tq2_{i}") for i in range(2)]
        HQ = [S(ctx, nc, f"HQ{i}") for i in range(2)]
        Yq = S(ctx, nc, "Yq")
        Sq = S(ctx, nc, "Sq")
        Gq = S(ctx, nc, "Gq")
        PE_ = S(ctx, nc, "PEm")
        VE = S(ctx, nc, "VEm")
        SE_ = S(ctx, nc, "SEm")
        GE = S(ctx, nc, "GEm")
        XR = S(ctx, nc, "XRm")
        PL = [S(ctx, nc, f"PL{i}") for i in range(5)]

        block = E(nc.Block())

        def YB(f):
            block.sync(f)

        def GB(f):
            block.gpsimd(f)

        def TB(f):
            block.tensor(f)

        def VB(f):
            block.vector(f)

        def SB(f):
            block.scalar(f)

        def stage_barrier():
            for mk in (YB, GB, TB, VB, SB):
                def _b(e):
                    for s in (PE_, VE, SE_, Lq, Sq, Gq, XR, Pq, Tq, Yq, *LP, *SR, *SPq,
                              *Tq2, *HQ):
                        if s.n:
                            e.wait_ge(s.h, s.n)
                mk(_b)

        def emit_ln(srcs, outs, pre_wait=None):
            P = srcs[0].shape[0]

            def _v1(v):
                v.wait_ge(SE_.h, SE_.n)
                if pre_wait is not None:
                    pre_wait(v)
                VE.inc(v.reduce_sum(mu_t[:P, :], srcs[0], axis=AX.X))
                if len(srcs) > 1:
                    VE.inc(v.reduce_sum(tm_t[:P, :], srcs[1], axis=AX.X))
                    v.wait_ge(VE.h, VE.n)
                    VE.inc(v.tensor_add(mu_t[:P, :], mu_t[:P, :], tm_t[:P, :]))
            VB(_v1)

            def _s1(s):
                if pre_wait is not None:
                    pre_wait(s)
                w0 = srcs[0].shape[-1]
                SE_.inc(s.activation(sqd[:P, 0:w0], srcs[0], AF.Square,
                                     accum_out=sa_t[:P, :]))
                if len(srcs) > 1:
                    w1 = srcs[1].shape[-1]
                    SE_.inc(s.activation(sqd[:P, w0:w0 + w1], srcs[1], AF.Square,
                                         accum_out=sb_t[:P, :]))
            SB(_s1)

            def _v2(v):
                v.wait_ge(SE_.h, SE_.n)
                v.wait_ge(VE.h, VE.n)
                if len(srcs) > 1:
                    VE.inc(v.tensor_add(sa_t[:P, :], sa_t[:P, :], sb_t[:P, :]))
                    v.wait_ge(VE.h, VE.n)
                nf = float(sum(s_.shape[-1] for s_ in srcs))
                VE.inc(v.tensor_scalar(mu_t[:P, :], mu_t[:P, :], 1.0 / nf, None, ALU.mult))
                VE.inc(v.tensor_scalar(ve_t[:P, :], sa_t[:P, :], 1.0 / nf, EPS, ALU.mult, ALU.add))
                v.wait_ge(VE.h, VE.n)
                VE.inc(v.scalar_tensor_tensor(tm_t[:P, :], mu_t[:P, :], -1.0, mu_t[:P, :],
                                              ALU.mult, ALU.mult))
                v.wait_ge(VE.h, VE.n)
                VE.inc(v.tensor_add(ve_t[:P, :], ve_t[:P, :], tm_t[:P, :]))
            VB(_v2)

            def _v2b(v):
                v.wait_ge(VE.h, VE.n)
                VE.inc(v.reciprocal(rstd_t[:P, :], ve_t[:P, :]))
            VB(_v2b)

            def _s2(s):
                s.wait_ge(VE.h, VE.n)
                SE_.inc(s.activation(rstd_t[:P, :], rstd_t[:P, :], AF.Sqrt))
            SB(_s2)

            def _v3(v):
                v.wait_ge(SE_.h, SE_.n)
                VE.inc(v.scalar_tensor_tensor(negb_t[:P, :], mu_t[:P, :], -1.0, rstd_t[:P, :],
                                              ALU.mult, ALU.mult))
            VB(_v3)

            def _s3(s):
                s.wait_ge(VE.h, VE.n)
                for o_ap, s_ap in outs:
                    SE_.inc(s.activation(o_ap, s_ap, AF.Identity,
                                         bias=negb_t[:P, :], scale=rstd_t[:P, :]))
            SB(_s3)

        def xch_phase(phase, write_pairs, read_pairs, wait_sem=None, wait_val=0,
                      extra_waits=()):
            if not comm:
                return
            gq_base = Gq.n

            def _g(gp):
                if wait_sem is not None and wait_val:
                    gp.wait_ge(wait_sem.h, wait_val)
                for sem_, val_ in extra_waits:
                    if val_:
                        gp.wait_ge(sem_.h, val_)
                gp.wait_ge(Pq.h, PRO_L)
                pid = nc.partition_id([EP.Pool])
                preg = gp.to_reg(gp.snap(pid))
                nd = len(write_pairs)
                for slot in range(8):
                    with gp.If_eq(preg, slot):
                        for fn, src in write_pairs:
                            gp.dma_start(out=fn(slot), in_=src).then_inc(Gq.h, 16)
                        gp.wait_ge(Gq.h, gq_base + nd * 16)
                        gp.dma_start(out=xch_flag[phase, slot:slot + 1].unsqueeze(0),
                                     in_=snonce[:]).then_inc(Gq.h, 16)
                Gq.n += (nd + 1) * 16
                nonce_reg = gp.alloc_register(f"xnon{phase}")
                gp.reg_load(nonce_reg, snonce[0:1, 0:1])
                flag_reg = gp.alloc_register(f"xflg{phase}")
                cond_reg = gp.alloc_register(f"xcnd{phase}")
                cnt_reg = gp.alloc_register(f"xcnt{phase}")
                gp.reg_mov(cond_reg, 1)
                gp.reg_mov(cnt_reg, 0)
                with gp.While(cond_reg):
                    gp.reg_add(cnt_reg, cnt_reg, 16)
                    for slot in range(8):
                        with gp.If_eq(preg, slot):
                            gp.dma_start(out=sflag[:],
                                         in_=xch_flag[phase, (slot ^ 1):(slot ^ 1) + 1].unsqueeze(0)
                                         ).then_inc(PL[phase].h, 16)
                    gp.wait_ge(PL[phase].h, cnt_reg)
                    gp.reg_load(flag_reg, sflag[0:1, 0:1])
                    gp.reg_alu(cond_reg, flag_reg, nonce_reg, ALU.not_equal)
                for slot in range(8):
                    with gp.If_eq(preg, slot):
                        for dst, fn in read_pairs:
                            gp.dma_start(out=dst, in_=fn(slot ^ 1)).then_inc(XR.h, 16)
                XR.n += len(read_pairs) * 16
            GB(_g)

        def _pro(sy):
            Pq.inc(sy.dma_start(out=idb[:], in_=idb_in), 16)
            Pq.inc(sy.dma_start(out=idf[:], in_=idf_in), 16)
            Pq.inc(sy.dma_start(out=onesf[:], in_=onesf_in), 16)
            Pq.inc(sy.dma_start(out=snonce[:], in_=nonce_in), 16)
            Pq.inc(sy.dma_start(out=shsc[:], in_=hs_in), 16)
        YB(_pro)
        PRO_L = Pq.n

        with contextlib.ExitStack() as spa:
            EA = spa.enter_context
            LNT = [EA(nc.sbuf_tensor(f"a_lnt{k}", [128, N], BF16)) for k in range(DT)]
            xld = [EA(nc.sbuf_tensor(f"p_xld{i}", [128, D], F32)) for i in range(2)]
            lnb = [EA(nc.sbuf_tensor(f"p_lnb{i}", [128, D], BF16)) for i in range(2)]

            norm_done = []
            st_done = []
            for g in range(2 * NT):
                src = x_my if g < NT else x_oth
                row0 = (g % NT) * 128
                bi = g % 2

                def _ld(sy):
                    if g >= 2:
                        sy.wait_ge(SE_.h, norm_done[g - 2])
                    LP[bi].inc(sy.dma_start(out=xld[bi][:], in_=src[row0:row0 + 128, :]), 16)
                YB(_ld)

                lw = LP[bi].n
                lsem = LP[bi]

                sq_now = Sq.n

                def _gate(e):
                    e.wait_ge(Pq.h, PRO_L)
                    e.wait_ge(lsem.h, lw)
                    if g >= 2:
                        e.wait_ge(Sq.h, sq_now)
                emit_ln([xld[bi][:]], [(lnb[bi][:], xld[bi][:])], pre_wait=_gate)
                norm_done.append(SE_.n)

                def _st(sy):
                    sy.wait_ge(SE_.h, SE_.n)
                    dst = row0 if g < NT else T + row0
                    Sq.inc(sy.dma_start(out=lnbuf[dst:dst + 128, :], in_=lnb[bi][:]), 16)
                    st_done.append(Sq.n)
                YB(_st)

            def _trld(sy):
                sy.wait_ge(Sq.h, Sq.n)
                for k in range(DT):
                    Lq.inc(sy.dma_start(out=LNT[k][:, :], in_=lnbuf[:, k * 128:(k + 1) * 128],
                                        transpose=True), 16)
            YB(_trld)
            LNT_L = Lq.n

            with contextlib.ExitStack() as sa:
                EB = sa.enter_context
                srow = [EB(nc.sbuf_tensor(f"a_srow{i}", [128, T], BF16)) for i in range(2)]
                dmp = [EB(nc.sbuf_tensor(f"a_dmp{i}", [128, T], BF16)) for i in range(2)]
                negm_jt = EB(nc.sbuf_tensor("a_negm", [128, 1], F32))
                pA = [EB(nc.psum_tensor(f"a_pA{i}", [128, 512], F32)) for i in range(2)]

                a_ev = []
                spill_done = []
                stats_done = []

                for jt in range(NJT):
                    sbx = jt % 2

                    for ic in range(8):
                        idx = jt * 8 + ic
                        pb = ic % 2

                        def _mm(t):
                            if jt == 0 and ic == 0:
                                t.wait_ge(Lq.h, LNT_L)
                            if idx >= 2:
                                t.wait_ge(VE.h, a_ev[idx - 2])
                            for kt in range(DT):
                                PE_.inc(t.matmul(pA[pb][:, :],
                                                 LNT[kt][:, jt * 128:(jt + 1) * 128],
                                                 LNT[kt][:, ic * 512:(ic + 1) * 512],
                                                 start=(kt == 0), stop=(kt == DT - 1)))
                        TB(_mm)
                        mm_pe = PE_.n

                        def _ev(v):
                            if ic == 0 and jt >= 2:
                                v.wait_ge(SE_.h, stats_done[jt - 2])
                            v.wait_ge(PE_.h, mm_pe)
                            VE.inc(v.tensor_copy(srow[sbx][:, ic * 512:(ic + 1) * 512],
                                                 pA[pb][:, :]))
                            a_ev.append(VE.n)
                        VB(_ev)

                    def _stat(v):
                        v.wait_ge(VE.h, VE.n)
                        v.wait_ge(SE_.h, SE_.n)
                        VE.inc(v.reduce_max(stM[:, jt:jt + 1], srow[sbx][:], axis=AX.X))
                        v.wait_ge(VE.h, VE.n)
                        VE.inc(v.tensor_scalar(negm_jt[:], stM[:, jt:jt + 1], -1.0, None, ALU.mult))
                    VB(_stat)

                    def _exp(s):
                        s.wait_ge(VE.h, VE.n)
                        if jt >= 2:
                            s.wait_ge(SPq[jt % 2].h, spill_done[jt - 2])
                        SE_.inc(s.activation(dmp[sbx][:, :], srow[sbx][:], AF.Exp,
                                             bias=negm_jt[:], accum_out=stZ[:, jt:jt + 1]))
                        stats_done.append(SE_.n)
                    SB(_exp)

                    def _sp(sy):
                        sy.wait_ge(SE_.h, stats_done[-1])
                        for ic in range(8):
                            SPq[jt % 2].inc(sy.dma_start(
                                out=sspill[jt, ic, :, :],
                                in_=dmp[sbx][:, ic * 512:(ic + 1) * 512]), 16)
                        spill_done.append(SPq[jt % 2].n)
                    YB(_sp)

                def _stflush(sy):
                    sy.wait_ge(SE_.h, stats_done[-1])
                    sy.wait_ge(VE.h, VE.n)
                YB(_stflush)

                xch_phase(
                    0,
                    [(lambda s: xch_stats[s, 0, :, :], stM[:, :]),
                     (lambda s: xch_stats[s, 1, :, :], stZ[:, :])],
                    [(pM[:, 0:32], lambda s: xch_stats[s, 0, :, 32:64]),
                     (pM[:, 32:64], lambda s: xch_stats[s, 0, :, 0:32]),
                     (pZ[:, 0:32], lambda s: xch_stats[s, 1, :, 32:64]),
                     (pZ[:, 32:64], lambda s: xch_stats[s, 1, :, 0:32])],
                    wait_sem=SE_, wait_val=stats_done[-1],
                    extra_waits=[(VE, VE.n)])

                def _comb(v):
                    v.wait_ge(XR.h, XR.n)
                    v.wait_ge(SE_.h, SE_.n)
                    if not comm:
                        VE.inc(v.tensor_copy(pM[:, :], stM[:, :]))
                        VE.inc(v.tensor_copy(pZ[:, :], stZ[:, :]))
                        v.wait_ge(VE.h, VE.n)
                    VE.inc(v.tensor_max(negMg[:, :], stM[:, :], pM[:, :]))
                    v.wait_ge(VE.h, VE.n)
                    VE.inc(v.tensor_sub(tmpw[:, :], stM[:, :], negMg[:, :]))
                    VE.inc(v.tensor_sub(pM[:, :], pM[:, :], negMg[:, :]))
                VB(_comb)

                def _combs(s):
                    s.wait_ge(VE.h, VE.n)
                    SE_.inc(s.activation(tmpw[:, :], tmpw[:, :], AF.Exp))
                    SE_.inc(s.activation(pM[:, :], pM[:, :], AF.Exp))
                SB(_combs)

                def _comb2(v):
                    v.wait_ge(SE_.h, SE_.n)
                    VE.inc(v.tensor_mul(stZ[:, :], stZ[:, :], tmpw[:, :]))
                    VE.inc(v.tensor_mul(pZ[:, :], pZ[:, :], pM[:, :]))
                    v.wait_ge(VE.h, VE.n)
                    VE.inc(v.tensor_add(stZ[:, :], stZ[:, :], pZ[:, :]))
                    v.wait_ge(VE.h, VE.n)
                    VE.inc(v.reciprocal(recipZ[:, :], stZ[:, :]))
                    VE.inc(v.tensor_scalar(negMg[:, :], negMg[:, :], -1.0, None, ALU.mult))
                    v.wait_ge(VE.h, VE.n)
                    VE.inc(v.tensor_mul(recipZ[:, :], recipZ[:, :], tmpw[:, :]))
                VB(_comb2)

        stage_barrier()

        with contextlib.ExitStack() as sb:
            EB = sb.enter_context
            XT = [EB(nc.sbuf_tensor(f"b_xt{j}", [128, D], BF16)) for j in range(NJT)]
            ssb = [EB(nc.sbuf_tensor(f"b_ssb{i}", [128, 512], BF16)) for i in range(8)]
            ctx_sb = [EB(nc.sbuf_tensor(f"b_ctx{i}", [128, D], F32)) for i in range(4)]
            ctx_cp = []
            xres = EB(nc.sbuf_tensor("b_xres", [128, D], F32))
            ctxf = EB(nc.sbuf_tensor("b_ctxf", [128, D], F32))
            xmid_sb = EB(nc.sbuf_tensor("b_xmid", [128, D], F32))
            y0_sb = EB(nc.sbuf_tensor("b_y0", [128, D], F32))
            pO = [EB(nc.psum_tensor(f"b_pO{i}", [128, 1024], F32)) for i in range(4)]

            def _xt_ld(sy):
                for j in range(NJT):
                    Lq.inc(sy.dma_start(out=XT[j][:, :], in_=lnbuf[j * 128:(j + 1) * 128, :]), 16)
            YB(_xt_ld)
            xt_l = Lq.n

            def _xt_sc(v):
                v.wait_ge(Lq.h, xt_l)
                for j in range(NJT):
                    VE.inc(v.tensor_scalar(XT[j][:, :], XT[j][:, :], recipZ[:, j:j + 1],
                                           None, ALU.mult))
            VB(_xt_sc)
            xt_ve = VE.n

            sp_rd = []
            mm_ato = []
            for ig in range(8):
                for jt in range(NJT):
                    idx = ig * NJT + jt
                    sbx = idx % 8

                    def _srd(sy):
                        if idx >= 8:
                            sy.wait_ge(PE_.h, mm_ato[idx - 8])
                        SR[sbx].inc(sy.dma_start(out=ssb[sbx][:, :],
                                                  in_=sspill[jt, ig, :, :]), 16)
                        sp_rd.append(SR[sbx].n)
                    YB(_srd)

                    def _smm(t):
                        t.wait_ge(SR[sbx].h, sp_rd[idx])
                        if idx == 0:
                            t.wait_ge(VE.h, xt_ve)
                        if jt == 0 and ig > 0:
                            t.wait_ge(VE.h, ctx_cp[ig - 1])
                        for gsub in range(4):
                            for half in range(2):
                                PE_.inc(t.matmul(
                                    pO[gsub][:, half * 512:(half + 1) * 512],
                                    ssb[sbx][:, gsub * 128:(gsub + 1) * 128],
                                    XT[jt][:, half * 512:(half + 1) * 512],
                                    start=(jt == 0), stop=(jt == NJT - 1)))
                        mm_ato.append(PE_.n)
                    TB(_smm)

                grp_pe = mm_ato[-1]

                def _gcp(v):
                    v.wait_ge(PE_.h, grp_pe)
                    if ig > 0:
                        v.wait_ge(SE_.h, SE_.n)
                    for gsub in range(4):
                        VE.inc(v.tensor_copy(ctx_sb[gsub][:, :], pO[gsub][:, :]))
                VB(_gcp)
                ctx_cp.append(VE.n)

                for gsub in range(4):
                    it = ig * 4 + gsub
                    row0 = it * 128

                    def _xl(sy):
                        sy.wait_ge(VE.h, VE.n)
                        sy.wait_ge(SE_.h, SE_.n)
                        Lq.inc(sy.dma_start(out=xres[:], in_=x_my[row0:row0 + 128, :]), 16)
                    YB(_xl)
                    xl_l = Lq.n
                    gcp_ve = ctx_cp[-1]

                    emit_ln([ctx_sb[gsub][:, :]], [(ctxf[:], ctx_sb[gsub][:, :])],
                            pre_wait=lambda e: e.wait_ge(VE.h, gcp_ve))

                    sq_b = Sq.n

                    def _mid(v):
                        v.wait_ge(SE_.h, SE_.n)
                        v.wait_ge(Lq.h, xl_l)
                        v.wait_ge(Sq.h, sq_b)
                        VE.inc(v.tensor_add(xmid_sb[:], xres[:], ctxf[:]))
                    VB(_mid)
                    mid_ve = VE.n

                    def _y0gate(e):
                        e.wait_ge(VE.h, mid_ve)
                        e.wait_ge(Sq.h, sq_b)
                    emit_ln([xmid_sb[:]], [(y0_sb[:], xmid_sb[:])], pre_wait=_y0gate)

                    def _wr(sy):
                        sy.wait_ge(SE_.h, SE_.n)
                        sy.wait_ge(VE.h, VE.n)
                        Sq.inc(sy.dma_start(out=xmidbuf[row0:row0 + 128, :], in_=xmid_sb[:]), 16)
                        Sq.inc(sy.dma_start(out=y0buf[513 + row0:513 + row0 + 128, :],
                                            in_=y0_sb[:]), 16)
                    YB(_wr)

            xmid_stores = Sq.n

        stage_barrier()

        xch_phase(1,
                  [(lambda s: xch_xmid[s, :, :], xmidbuf[T - 513:T, :])],
                  [(y2buf[0:513, :], lambda s: xch_xmid[s, :, :])],
                  wait_sem=Sq, wait_val=xmid_stores)
        if not comm:
            def _zf(v):
                VE.inc(v.memset(halot[:, :], 0.0))
            VB(_zf)

            def _zs(sy):
                sy.wait_ge(VE.h, VE.n)
                for r0, nr in [(0, 128), (128, 128), (256, 128), (384, 128), (512, 1)]:
                    Sq.inc(sy.dma_start(out=y2buf[r0:r0 + nr, :], in_=halot[:nr, :]), 16)
            YB(_zs)

        with contextlib.ExitStack() as sh:
            halo_tiles = [(0, 128), (128, 128), (256, 128), (384, 128), (512, 1)]
            for r0, nr in halo_tiles:
                def _hld(sy):
                    sy.wait_ge(XR.h, XR.n)
                    sy.wait_ge(SE_.h, SE_.n)
                    sy.wait_ge(VE.h, VE.n)
                    sy.wait_ge(Sq.h, Sq.n)
                    Yq.inc(sy.dma_start(out=halot[:nr, :], in_=y2buf[r0:r0 + nr, :]), 16)
                YB(_hld)
                hl = Yq.n

                def _hsc(v):
                    v.wait_ge(Yq.h, hl)
                    v.wait_ge(SE_.h, SE_.n)
                    VE.inc(v.tensor_scalar(halot[:nr, :], halot[:nr, :], shsc[:nr, :],
                                           None, ALU.mult))
                VB(_hsc)
                hv = VE.n

                emit_ln([halot[:nr, :]], [(halot[:nr, :], halot[:nr, :])],
                        pre_wait=lambda e, hv=hv: e.wait_ge(VE.h, hv))

                def _hst(sy):
                    sy.wait_ge(SE_.h, SE_.n)
                    Sq.inc(sy.dma_start(out=y0buf[r0:r0 + nr, :], in_=halot[:nr, :]), 16)
                YB(_hst)

        stage_barrier()

        def run_g1(layer, ybin, ybout, xchrow, phase_row):
            with contextlib.ExitStack() as sg:
                EG = sg.enter_context
                wqkv_sb = [EG(nc.sbuf_tensor(f"g1{layer}_wqkv{k}", [128, QKV], BF16)) for k in range(DT)]
                wo_sb = [EG(nc.sbuf_tensor(f"g1{layer}_wo{c}", [128, D], BF16)) for c in range(4)]
                tqc_sb = [EG(nc.sbuf_tensor(f"g1{layer}_tqc{x}", [128, 4 * DH], F32)) for x in range(2)]
                tqs_sb = [EG(nc.sbuf_tensor(f"g1{layer}_tqs{x}", [128, 4 * DH], F32)) for x in range(2)]
                tkc_sb = [EG(nc.sbuf_tensor(f"g1{layer}_tkc{x}", [128, 4 * DH], F32)) for x in range(2)]
                tks_sb = [EG(nc.sbuf_tensor(f"g1{layer}_tks{x}", [128, 4 * DH], F32)) for x in range(2)]
                bigm_sb = EG(nc.sbuf_tensor(f"g1{layer}_bigm", [128, 1408], BF16))
                mw0_sb = EG(nc.sbuf_tensor(f"g1{layer}_mw0", [128, 8 * WIN], BF16))
                qsrow = EG(nc.sbuf_tensor(f"g1{layer}_qsrow", [1, DH], F32))
                ksrow = EG(nc.sbuf_tensor(f"g1{layer}_ksrow", [1, DH], F32))
                qs_rep = EG(nc.sbuf_tensor(f"g1{layer}_qsrep", [128, DH], F32))
                ks_rep = EG(nc.sbuf_tensor(f"g1{layer}_ksrep", [128, DH], F32))
                hsb = [[EG(nc.sbuf_tensor(f"g1{layer}_hsb{x}_{i}", [128, D], BF16))
                        for i in range(4)] for x in range(2)]
                hsT = [EG(nc.sbuf_tensor(f"g1{layer}_hsT{k}", [128, WIN], BF16)) for k in range(DT)]
                qf = [EG(nc.sbuf_tensor(f"g1{layer}_qf{x}", [128, INNER], F32)) for x in range(2)]
                kf = [EG(nc.sbuf_tensor(f"g1{layer}_kf{x}", [128, INNER], F32)) for x in range(2)]
                rot = [EG(nc.sbuf_tensor(f"g1{layer}_rot{x}", [128, INNER], F32)) for x in range(2)]
                t1b = [EG(nc.sbuf_tensor(f"g1{layer}_t1{x}", [128, INNER], F32)) for x in range(2)]
                nsb = [EG(nc.sbuf_tensor(f"g1{layer}_ns{x}", [128, HEADS], F32)) for x in range(2)]
                qrb = [EG(nc.sbuf_tensor(f"g1{layer}_qrb{x}", [128, INNER], BF16)) for x in range(2)]
                krb = [EG(nc.sbuf_tensor(f"g1{layer}_krb{x}", [128, INNER], BF16)) for x in range(2)]
                qT = [EG(nc.sbuf_tensor(f"g1{layer}_qT{p}", [128, WIN], BF16)) for p in range(4)]
                kT = [[EG(nc.sbuf_tensor(f"g1{layer}_kT{x}_{p}", [128, WIN], BF16)) for p in range(4)]
                      for x in range(2)]
                v65 = [[EG(nc.sbuf_tensor(f"g1{layer}_v65{x}_{tt}", [128, HEADS * 65], BF16))
                        for tt in range(4)] for x in range(2)]
                simT = [EG(nc.sbuf_tensor(f"g1{layer}_simT{i}", [128, 8 * WIN], F32)) for i in range(2)]
                PT = [EG(nc.sbuf_tensor(f"g1{layer}_PT{i}", [128, 8 * WIN], BF16)) for i in range(2)]
                kmax1 = EG(nc.sbuf_tensor(f"g1{layer}_kmax", [128, WIN], F32))
                kmax = [kmax1, kmax1]
                kmaxb = [EG(nc.sbuf_tensor(f"g1{layer}_kmaxb{i}", [128, WIN], BF16)) for i in range(2)]
                kmT1 = EG(nc.sbuf_tensor(f"g1{layer}_kmT", [128, WIN], BF16))
                kmT = [kmT1, kmT1]
                negq = [EG(nc.sbuf_tensor(f"g1{layer}_negq{i}", [128, 4], F32)) for i in range(2)]
                colrep1 = EG(nc.sbuf_tensor(f"g1{layer}_colrep", [128, WIN], BF16))
                colrep4 = [colrep1, colrep1]
                negrep1 = EG(nc.sbuf_tensor(f"g1{layer}_negrep", [128, WIN], BF16))
                negrep = [negrep1, negrep1]
                rz1 = EG(nc.sbuf_tensor(f"g1{layer}_rz", [128, 4], F32))
                rz4 = [rz1, rz1]
                av1 = EG(nc.sbuf_tensor(f"g1{layer}_av", [128, 256], BF16))
                av4 = [av1, av1]
                avT = [EG(nc.sbuf_tensor(f"g1{layer}_avT{c}", [128, WIN], BF16)) for c in range(4)]
                y0f = [EG(nc.sbuf_tensor(f"g1{layer}_y0f{i}", [128, D], F32)) for i in range(2)]
                y1t = [EG(nc.sbuf_tensor(f"g1{layer}_y1t{i}", [128, D], F32)) for i in range(2)]

                pq = [EG(nc.psum_tensor(f"g1{layer}_pq{i}", [128, 512], F32)) for i in range(2)]
                psim = [EG(nc.psum_tensor(f"g1{layer}_ps{i}", [128, 512], F32)) for i in range(4)]
                pav = EG(nc.psum_tensor(f"g1{layer}_pav", [128, 260], F32))
                ptb = EG(nc.psum_tensor(f"g1{layer}_ptb", [128, 512], BF16))

                def _wl(gp):
                    for s in (PE_, VE, SE_):
                        if s.n:
                            gp.wait_ge(s.h, s.n)
                    for k in range(DT):
                        Gq.inc(gp.dma_start(out=wqkv_sb[k][:, :],
                                            in_=wqkv_in[layer, k * 128:(k + 1) * 128, :]), 16)
                    for c in range(4):
                        Gq.inc(gp.dma_start(out=wo_sb[c][:, :],
                                            in_=wo_in[layer, c * 128:(c + 1) * 128, :]), 16)
                GB(_wl)

                def _tl(sy):
                    for s in (PE_, VE, SE_):
                        if s.n:
                            sy.wait_ge(s.h, s.n)
                    Lq.inc(sy.dma_start(out=bigm_sb[:], in_=bigmask_in), 16)
                    for kt in range(8):
                        Lq.inc(sy.dma_start(out=mw0_sb[:, kt * WIN:(kt + 1) * WIN],
                                            in_=maskw0_in[kt, :, :]), 16)
                    Lq.inc(sy.dma_start(out=qsrow[:], in_=qs_in[layer:layer + 1, :]), 16)
                    Lq.inc(sy.dma_start(out=ksrow[:], in_=ks_in[layer:layer + 1, :]), 16)
                YB(_tl)
                tl_l = Lq.n

                def _rep(t):
                    t.wait_ge(Lq.h, tl_l)
                    PE_.inc(t.matmul(pav[:, 0:DH], onesf[0:1, :], qsrow[0:1, :], start=True, stop=True))
                TB(_rep)

                def _repv(v):
                    v.wait_ge(PE_.h, PE_.n)
                    VE.inc(v.tensor_copy(qs_rep[:, :], pav[:, 0:DH]))
                VB(_repv)

                def _rep2(t):
                    t.wait_ge(VE.h, VE.n)
                    PE_.inc(t.matmul(pav[:, 65:65 + DH], onesf[0:1, :], ksrow[0:1, :],
                                     start=True, stop=True))
                TB(_rep2)

                def _rep2v(v):
                    v.wait_ge(PE_.h, PE_.n)
                    VE.inc(v.tensor_copy(ks_rep[:, :], pav[:, 65:65 + DH]))
                VB(_rep2v)
                fold_ve = VE.n

                def _vo(v):
                    for x in range(2):
                        for tt in range(4):
                            VE.inc(v.memset(v65[x][tt][:, :], 1.0))
                VB(_vo)
                g1_w_gq = Gq.n

                rot_done = {}
                htr_pe = {}
                res_ve = {}
                vev_se = {}
                qkv_pe = {}
                attn_pe = {}
                o_pe = {}
                avmm_pe = [0, 0]
                exp_se = [0, 0]
                ptb_free = [0]
                pav_free = [0]

                for cw in range(-1, 8):
                    cur = (cw + 1) % 2
                    prv = cw % 2
                    kv_only = (cw == -1)
                    base = 513 + cw * WIN

                    def _cl(gp):
                        gp.wait_ge(Gq.h, g1_w_gq)
                        if cw - 2 in htr_pe:
                            gp.wait_ge(PE_.h, htr_pe[cw - 2])
                        for tt in range(4):
                            r = base + tt * 128
                            HQ[cur].inc(gp.dma_start(out=hsb[cur][tt][:, 0:512],
                                                     in_=ybin[r:r + 128, 0:512]), 16)
                            HQ[cur].inc(gp.dma_start(out=hsb[cur][tt][:, 512:1024],
                                                     in_=ybin[r - 1:r + 127, 512:1024]), 16)
                    GB(_cl)
                    cl_g = HQ[cur].n

                    trow = (cw + 1) * 512
                    tpar = (cw + 1) % 2

                    def _ctl(sy):
                        if cw - 2 in rot_done:
                            sy.wait_ge(VE.h, rot_done[cw - 2])
                        tc0 = (cw + 1) * 4 * DH
                        for tin, tsb in ((tq_c, tqc_sb[tpar]), (tq_s, tqs_sb[tpar]),
                                         (tk_c, tkc_sb[tpar]), (tk_s, tks_sb[tpar])):
                            Tq2[tpar].inc(sy.dma_start(
                                out=tsb[:, :], in_=tin[:, tc0:tc0 + 4 * DH]), 16)
                    YB(_ctl)
                    ctl_l = Tq2[tpar].n

                    def _cfold(v):
                        v.wait_ge(Tq2[tpar].h, ctl_l)
                        if cw == -1:
                            v.wait_ge(VE.h, fold_ve)
                        for tc, srcrep in ((tqc_sb[tpar], qs_rep), (tkc_sb[tpar], ks_rep)):
                            t3 = tc[:].rearrange("p (t c) -> p t c", c=DH)
                            VE.inc(v.tensor_mul(t3, t3,
                                                srcrep[:].unsqueeze(1).broadcast_to((128, 4, DH))))
                        for ts, srcrep in ((tqs_sb[tpar], qs_rep), (tks_sb[tpar], ks_rep)):
                            t3 = ts[:].rearrange("p (t c) -> p t c", c=DH)
                            VE.inc(v.tensor_mul(t3[:, :, 0:32], t3[:, :, 0:32],
                                                srcrep[:, 32:64].unsqueeze(1).broadcast_to((128, 4, 32))))
                            VE.inc(v.tensor_mul(t3[:, :, 32:64], t3[:, :, 32:64],
                                                srcrep[:, 0:32].unsqueeze(1).broadcast_to((128, 4, 32))))
                    VB(_cfold)
                    fold_done = VE.n

                    hsT_ev = 0
                    for grp in range(8):
                        tt, dh4 = grp // 2, (grp % 2) * 4

                        def _tg(t):
                            if grp == 0:
                                t.wait_ge(HQ[cur].h, cl_g)
                                if cw == -1:
                                    t.wait_ge(Gq.h, g1_w_gq)
                            t.wait_ge(VE.h, ptb_free[0])
                            for j in range(4):
                                PE_.inc(t.transpose(ptb[:, j * 128:(j + 1) * 128],
                                                    hsb[cur][tt][:, (dh4 + j) * 128:(dh4 + j + 1) * 128],
                                                    idb[:]))
                        TB(_tg)
                        tg_pe = PE_.n

                        def _te(v):
                            if grp == 0 and cw - 1 in qkv_pe:
                                v.wait_ge(PE_.h, qkv_pe[cw - 1])
                            v.wait_ge(PE_.h, tg_pe)
                            for j in range(4):
                                VE.inc(v.tensor_copy(hsT[dh4 + j][:, tt * 128:(tt + 1) * 128],
                                                     ptb[:, j * 128:(j + 1) * 128]))
                            ptb_free[0] = VE.n
                        VB(_te)
                    htr_pe[cw] = PE_.n
                    hsT_ev = VE.n

                    qev_done = {}
                    kvtr_ve = 0
                    for tt in range(4):
                        tp2 = tt % 2
                        nb_list = [1, 2] if kv_only else [0, 1, 2]

                        qev_mid = [0]

                        def _qmm1(t):
                            t.wait_ge(VE.h, hsT_ev)
                            if tt == 0 and cw - 1 in res_ve:
                                t.wait_ge(VE.h, res_ve[cw - 1])
                            if tt > 0:
                                t.wait_ge(VE.h, qev_done[tt - 1][0])
                                t.wait_ge(SE_.h, qev_done[tt - 1][1])
                            if cw == -1 and tt == 0:
                                t.wait_ge(VE.h, fold_done)
                            if not kv_only:
                                for kt in range(DT):
                                    PE_.inc(t.matmul(pq[0][:, :],
                                                     hsT[kt][:, tt * 128:(tt + 1) * 128],
                                                     wqkv_sb[kt][:, 0:512],
                                                     start=(kt == 0), stop=(kt == DT - 1)))
                            for kt in range(DT):
                                PE_.inc(t.matmul(pq[1][:, :],
                                                 hsT[kt][:, tt * 128:(tt + 1) * 128],
                                                 wqkv_sb[kt][:, 512:1024],
                                                 start=(kt == 0), stop=(kt == DT - 1)))
                        TB(_qmm1)
                        qk_pe = PE_.n

                        def _qev(v):
                            v.wait_ge(PE_.h, qk_pe)
                            if not kv_only:
                                VE.inc(v.tensor_copy(qf[tp2][:, :], pq[0][:, :]))
                            VE.inc(v.tensor_copy(kf[tp2][:, :], pq[1][:, :]))
                            qev_mid[0] = VE.n
                        VB(_qev)

                        def _vmm(t):
                            t.wait_ge(VE.h, qev_mid[0])
                            for kt in range(DT):
                                PE_.inc(t.matmul(pq[0][:, :],
                                                 hsT[kt][:, tt * 128:(tt + 1) * 128],
                                                 wqkv_sb[kt][:, 1024:1536],
                                                 start=(kt == 0), stop=(kt == DT - 1)))
                        TB(_vmm)
                        v_pe = PE_.n

                        def _vev(s):
                            s.wait_ge(PE_.h, v_pe)
                            if tt == 0 and cw - 1 in attn_pe:
                                s.wait_ge(PE_.h, attn_pe[cw - 1])
                            SE_.inc(s.copy(
                                v65[cur][tt][:].rearrange("p (h s) -> p h s", s=65)[:, :, 0:DH],
                                pq[0][:].rearrange("p (h c) -> p h c", c=DH)))
                        SB(_vev)
                        vev_now = SE_.n

                        tcol = tt * DH

                        def _rot_pipeline(srcl, tc_sb, ts_sb, outb):
                            srcf = srcl[tp2]
                            def _r1(v):
                                v.wait_ge(VE.h, qev_mid[0])
                                src3 = srcf[:].rearrange("p (h c) -> p h c", c=DH)
                                VE.inc(v.tensor_mul(sqd[:, 0:INNER].rearrange("p (h c) -> p h c", c=DH),
                                                    src3, src3))
                                v.wait_ge(VE.h, VE.n)
                                VE.inc(v.reduce_sum(nsb[tp2][:].unsqueeze(2),
                                                    sqd[:, 0:INNER].rearrange("p (h c) -> p h c", c=DH),
                                                    axis=AX.X))
                                v.wait_ge(VE.h, VE.n)
                                VE.inc(v.tensor_scalar_max(nsb[tp2][:], nsb[tp2][:], 1e-24))
                            VB(_r1)

                            def _r2v(v):
                                v.wait_ge(VE.h, VE.n)
                                VE.inc(v.reciprocal(nsb[tp2][:], nsb[tp2][:]))
                            VB(_r2v)

                            def _r2(s):
                                s.wait_ge(VE.h, VE.n)
                                SE_.inc(s.activation(nsb[tp2][:], nsb[tp2][:], AF.Sqrt))
                            SB(_r2)

                            def _r3(v):
                                v.wait_ge(SE_.h, SE_.n)
                                src3 = srcf[:].rearrange("p (h c) -> p h c", c=DH)
                                VE.inc(v.tensor_mul(src3, src3,
                                                    nsb[tp2][:].unsqueeze(2).broadcast_to((128, HEADS, DH))))
                                v.wait_ge(VE.h, VE.n)
                                rot3 = rot[tp2][:].rearrange("p (h c) -> p h c", c=DH)
                                VE.inc(v.tensor_scalar(rot3[:, :, 0:32], src3[:, :, 32:64],
                                                       -1.0, None, ALU.mult))
                                VE.inc(v.tensor_copy(rot3[:, :, 32:64], src3[:, :, 0:32]))
                                tc3 = tc_sb[:, tcol:tcol + DH].unsqueeze(1).broadcast_to((128, HEADS, DH))
                                ts3 = ts_sb[:, tcol:tcol + DH].unsqueeze(1).broadcast_to((128, HEADS, DH))
                                t13 = t1b[tp2][:].rearrange("p (h c) -> p h c", c=DH)
                                VE.inc(v.tensor_mul(t13, src3, tc3))
                                v.wait_ge(VE.h, VE.n)
                                VE.inc(v.tensor_mul(rot3, rot3, ts3))
                                v.wait_ge(VE.h, VE.n)
                                VE.inc(v.tensor_tensor(outb[tp2][:].rearrange("p (h c) -> p h c", c=DH),
                                                       t13, rot3, ALU.add))
                            VB(_r3)

                        _rot_pipeline(kf, tkc_sb[tpar], tks_sb[tpar], krb)
                        if not kv_only:
                            _rot_pipeline(qf, tqc_sb[tpar], tqs_sb[tpar], qrb)
                        rotk_ve = VE.n

                        def _tk(t):
                            t.wait_ge(VE.h, rotk_ve)
                            t.wait_ge(VE.h, ptb_free[0])
                            for hp in range(4):
                                PE_.inc(t.transpose(ptb[:, hp * 128:(hp + 1) * 128],
                                                    krb[tp2][:, hp * 128:(hp + 1) * 128], idb[:]))
                        TB(_tk)
                        tk_pe = PE_.n

                        def _ek(v):
                            if tt == 0 and cw - 1 in attn_pe:
                                v.wait_ge(PE_.h, attn_pe[cw - 1])
                            v.wait_ge(PE_.h, tk_pe)
                            for hp in range(4):
                                VE.inc(v.tensor_copy(kT[cur][hp][:, tt * 128:(tt + 1) * 128],
                                                     ptb[:, hp * 128:(hp + 1) * 128]))
                            ptb_free[0] = VE.n
                        VB(_ek)

                        if not kv_only:
                            def _tq(t):
                                t.wait_ge(VE.h, ptb_free[0])
                                for hp in range(4):
                                    PE_.inc(t.transpose(ptb[:, hp * 128:(hp + 1) * 128],
                                                        qrb[tp2][:, hp * 128:(hp + 1) * 128], idb[:]))
                            TB(_tq)
                            tq_pe = PE_.n

                            def _eq(v):
                                v.wait_ge(PE_.h, tq_pe)
                                for hp in range(4):
                                    VE.inc(v.tensor_copy(qT[hp][:, tt * 128:(tt + 1) * 128],
                                                         ptb[:, hp * 128:(hp + 1) * 128]))
                                ptb_free[0] = VE.n
                            VB(_eq)

                        qev_done[tt] = (VE.n, vev_now)

                    rot_done[cw] = VE.n
                    vev_se[cw] = SE_.n
                    qkv_pe[cw] = PE_.n
                    kv_ready = VE.n

                    if kv_only:
                        attn_pe[cw] = PE_.n
                        o_pe[cw] = PE_.n
                        continue

                    sim_ev = []

                    for hp in range(4):
                        for kt in range(8):
                            ksrc = kT[prv][hp] if kt < 4 else kT[cur][hp]
                            kcol = (kt % 4) * 128
                            bk = (len(sim_ev) // 2) % 2

                            def _smm(t):
                                if not sim_ev:
                                    t.wait_ge(VE.h, kv_ready)
                                if len(sim_ev) >= 4:
                                    t.wait_ge(VE.h, sim_ev[-3])
                                PE_.inc(t.matmul(psim[2 * bk][:, :], ksrc[0:64, kcol:kcol + 128],
                                                 qT[hp][0:64, :], start=True, stop=True,
                                                 tile_position=(0, 0)))
                                PE_.inc(t.matmul(psim[2 * bk + 1][:, :], ksrc[64:128, kcol:kcol + 128],
                                                 qT[hp][64:128, :], start=True, stop=True,
                                                 tile_position=(64, 0)))
                            TB(_smm)
                            smm_pe = PE_.n

                            def _sev(v):
                                if kt == 0:
                                    v.wait_ge(SE_.h, max(exp_se))
                                v.wait_ge(PE_.h, smm_pe)
                                for sdx in range(2):
                                    if cw == 0:
                                        msk = mw0_sb[:, kt * WIN:(kt + 1) * WIN]
                                    else:
                                        c0 = 896 - kt * 128
                                        msk = bigm_sb[:, c0:c0 + WIN]
                                    VE.inc(v.scalar_tensor_tensor(
                                        simT[sdx][:, kt * WIN:(kt + 1) * WIN],
                                        psim[2 * bk + sdx][:, :], 8.0, msk, ALU.mult, ALU.add))
                                    sim_ev.append(VE.n)
                            VB(_sev)

                        pair_ev = sim_ev[-1]

                        def _mx2(v):
                            v.wait_ge(VE.h, pair_ev)
                            for sdx in range(2):
                                VE.inc(v.reduce_max(
                                    kmax[sdx][:].unsqueeze(2),
                                    simT[sdx][:].rearrange("p (k q) -> p q k", k=8),
                                    axis=AX.X))
                                v.wait_ge(VE.h, VE.n)
                                VE.inc(v.tensor_copy(kmaxb[sdx][:, :], kmax[sdx][:, :]))
                                v.wait_ge(VE.h, VE.n)
                        VB(_mx2)
                        mx_ve = VE.n

                        rmx_ve = [0, 0]
                        for sdx in range(2):
                            def _trm(t):
                                t.wait_ge(VE.h, mx_ve)
                                t.wait_ge(VE.h, ptb_free[0])
                                for qt in range(4):
                                    PE_.inc(t.transpose(ptb[:, qt * 128:(qt + 1) * 128],
                                                        kmaxb[sdx][:, qt * 128:(qt + 1) * 128], idb[:]))
                            TB(_trm)
                            trm_pe = PE_.n

                            def _rmx(v):
                                v.wait_ge(PE_.h, trm_pe)
                                v.wait_ge(VE.h, VE.n)
                                VE.inc(v.tensor_copy(kmT[sdx][:, :], ptb[:, :]))
                                ptb_free[0] = VE.n
                                v.wait_ge(VE.h, VE.n)
                                VE.inc(v.reduce_max(negq[sdx][:].unsqueeze(2),
                                                    kmT[sdx][:].rearrange("p (t j) -> p t j", t=4),
                                                    axis=AX.X, negate=True))
                            VB(_rmx)
                            rmx_ve[sdx] = VE.n

                        sub_ve = [0, 0]
                        sub_g = [0]
                        pt_se2 = [0, 0]
                        trc_pe2 = [0, 0]
                        for sdx in range(2):
                            def _crep(v):
                                v.wait_ge(VE.h, rmx_ve[sdx])
                                if sdx == 1:
                                    v.wait_ge(PE_.h, trc_pe2[0])
                                for qt in range(4):
                                    VE.inc(v.tensor_scalar(colrep4[sdx][:, qt * 128:(qt + 1) * 128],
                                                           kmaxb[sdx][:, 0:128], 0.0,
                                                           negq[sdx][:, qt:qt + 1],
                                                           ALU.mult, ALU.add))
                            VB(_crep)
                            crep_ve = VE.n

                            def _trc(t):
                                t.wait_ge(VE.h, crep_ve)
                                t.wait_ge(VE.h, ptb_free[0])
                                for qt in range(4):
                                    PE_.inc(t.transpose(ptb[:, qt * 128:(qt + 1) * 128],
                                                        colrep4[sdx][:, qt * 128:(qt + 1) * 128], idb[:]))
                            TB(_trc)
                            trc_pe = PE_.n
                            trc_pe2[sdx] = PE_.n

                            def _erc(v):
                                v.wait_ge(PE_.h, trc_pe)
                                v.wait_ge(VE.h, VE.n)
                                VE.inc(v.tensor_copy(negrep[sdx][:, :], ptb[:, :]))
                                ptb_free[0] = VE.n
                                v.wait_ge(VE.h, VE.n)
                                VE.inc(v.tensor_tensor(
                                    simT[sdx][:].rearrange("p (k q) -> p k q", k=8),
                                    simT[sdx][:].rearrange("p (k q) -> p k q", k=8),
                                    negrep[sdx][:].unsqueeze(1).broadcast_to((128, 8, WIN)),
                                    ALU.add))
                            VB(_erc)
                            sub_ve[sdx] = VE.n

                            def _exp(s):
                                s.wait_ge(VE.h, sub_ve[sdx])
                                s.wait_ge(PE_.h, avmm_pe[sdx])
                                SE_.inc(s.activation(PT[sdx][:, :], simT[sdx][:, :], AF.Exp))
                            SB(_exp)
                            pt_se2[sdx] = SE_.n
                            exp_se[sdx] = SE_.n

                        avz_ve = [0, 0]
                        for sdx in range(2):
                            h = hp * 2 + sdx

                            def _avm(t):
                                t.wait_ge(SE_.h, pt_se2[sdx])
                                t.wait_ge(VE.h, pav_free[0])
                                for qt in range(4):
                                    for kt in range(8):
                                        vsrc = v65[prv][kt % 4] if kt < 4 else v65[cur][kt % 4]
                                        PE_.inc(t.matmul(
                                            pav[:, qt * 65:(qt + 1) * 65],
                                            PT[sdx][:, kt * WIN + qt * 128: kt * WIN + (qt + 1) * 128],
                                            vsrc[:, h * 65:(h + 1) * 65],
                                            start=(kt == 0), stop=(kt == 7)))
                            TB(_avm)
                            avmm_pe[sdx] = PE_.n

                            def _avz(v):
                                v.wait_ge(PE_.h, avmm_pe[sdx])
                                v.wait_ge(VE.h, VE.n)
                                pv = pav[:].rearrange("p (q s) -> p q s", s=65)
                                VE.inc(v.reciprocal(rz4[sdx][:].unsqueeze(2), pv[:, :, 64:65]))
                                v.wait_ge(VE.h, VE.n)
                                VE.inc(v.tensor_tensor(av4[sdx][:].rearrange("p (q c) -> p q c", c=DH),
                                                       pv[:, :, 0:DH],
                                                       rz4[sdx][:].unsqueeze(2).broadcast_to((128, 4, DH)),
                                                       ALU.mult))
                                pav_free[0] = VE.n
                            VB(_avz)
                            avz_ve[sdx] = VE.n

                            def _avt(t):
                                t.wait_ge(VE.h, avz_ve[sdx])
                                t.wait_ge(VE.h, ptb_free[0])
                                for qt in range(4):
                                    PE_.inc(t.transpose(ptb[0:DH, qt * 128:(qt + 1) * 128],
                                                        av4[sdx][:, qt * DH:(qt + 1) * DH], idb[:]))
                            TB(_avt)
                            avt_pe = PE_.n

                            def _ave(v):
                                if hp == 0 and sdx == 0 and cw - 1 in o_pe:
                                    v.wait_ge(PE_.h, o_pe[cw - 1])
                                v.wait_ge(PE_.h, avt_pe)
                                VE.inc(v.tensor_copy(avT[hp][sdx * DH:(sdx + 1) * DH, :],
                                                     ptb[0:DH, :]))
                                ptb_free[0] = VE.n
                            VB(_ave)

                    attn_pe[cw] = PE_.n
                    avT_ve = VE.n

                    for tt in range(4):
                        r = base + tt * 128
                        ybi = tt % 2

                        def _ylf(sy):
                            if cw - 1 in res_ve:
                                sy.wait_ge(VE.h, res_ve[cw - 1])
                            if tt >= 2:
                                sy.wait_ge(VE.h, VE.n)
                            Yq.inc(sy.dma_start(out=y0f[ybi][:, :], in_=ybin[r:r + 128, :]), 16)
                        YB(_ylf)
                        ylf_l = Yq.n

                        def _omm(t):
                            t.wait_ge(VE.h, avT_ve if tt == 0 else VE.n)
                            for ct in range(4):
                                for nb in range(2):
                                    PE_.inc(t.matmul(pq[nb][:, :],
                                                     avT[ct][:, tt * 128:(tt + 1) * 128],
                                                     wo_sb[ct][:, nb * 512:(nb + 1) * 512],
                                                     start=(ct == 0), stop=(ct == 3)))
                        TB(_omm)
                        omm_pe = PE_.n

                        def _res(v):
                            v.wait_ge(PE_.h, omm_pe)
                            v.wait_ge(Yq.h, ylf_l)
                            v.wait_ge(Sq.h, Sq.n)
                            VE.inc(v.tensor_add(y1t[ybi][:, 0:512], y0f[ybi][:, 0:512], pq[0][:, :]))
                            VE.inc(v.tensor_add(y1t[ybi][:, 512:1024], y0f[ybi][:, 512:1024],
                                                pq[1][:, :]))
                        VB(_res)

                        def _yst(sy):
                            sy.wait_ge(VE.h, VE.n)
                            ro = 1 + cw * WIN + tt * 128
                            Sq.inc(sy.dma_start(out=ybout[ro:ro + 128, :], in_=y1t[ybi][:, :]), 16)
                        YB(_yst)
                    res_ve[cw] = VE.n
                    o_pe[cw] = PE_.n

                g1_stores = Sq.n
                xch_phase(phase_row,
                          [(lambda s: xchrow[s, :, :], ybout[T:T + 1, :])],
                          [(halorow[:, :], lambda s: xchrow[s, :, :])],
                          wait_sem=Sq, wait_val=g1_stores)

                def _hrow(v):
                    v.wait_ge(XR.h, XR.n)
                    if not comm:
                        VE.inc(v.memset(halorow[:, :], 0.0))
                        v.wait_ge(VE.h, VE.n)
                    VE.inc(v.tensor_scalar(halorow[:, :], halorow[:, :], shsc[0:1, :], None,
                                           ALU.mult))
                VB(_hrow)

                def _hrst(sy):
                    sy.wait_ge(VE.h, VE.n)
                    Sq.inc(sy.dma_start(out=ybout[0:1, :], in_=halorow[:, :]), 16)
                YB(_hrst)

            stage_barrier()

        def run_g2(layer, ybin, ybout, tail_phase, tail_xch, out_base):
            with contextlib.ExitStack() as sg:
                EG = sg.enter_context
                w1_sb = [EG(nc.sbuf_tensor(f"g2{layer}_w1{k}", [128, HID2], BF16)) for k in range(DT)]
                w2_sb = [EG(nc.sbuf_tensor(f"g2{layer}_w2{htx}", [HT_SIZES[htx], D], BF16))
                         for htx in range(NHT)]
                b1aT = EG(nc.sbuf_tensor(f"g2{layer}_b1aT", [128, NHT], F32))
                b1gT = EG(nc.sbuf_tensor(f"g2{layer}_b1gT", [128, NHT], F32))
                lg_rep = EG(nc.sbuf_tensor(f"g2{layer}_lgrep", [128, D], BF16))
                lb_rep = EG(nc.sbuf_tensor(f"g2{layer}_lbrep", [128, D], BF16))
                b2rep = EG(nc.sbuf_tensor(f"g2{layer}_b2rep", [128, D], F32))
                btmp = EG(nc.sbuf_tensor(f"g2{layer}_btmp", [22, 128], F32))
                b2row = halot[0:1, :]
                y1f = [EG(nc.sbuf_tensor(f"g2{layer}_y1f{i}", [128, D], F32)) for i in range(2)]
                y1s_ = EG(nc.sbuf_tensor(f"g2{layer}_y1s", [128, 512], F32))
                y1s = [y1s_, y1s_]
                hnb = EG(nc.sbuf_tensor(f"g2{layer}_hnb", [128, D], BF16))
                hlnT = [EG(nc.sbuf_tensor(f"g2{layer}_hlnT{k}", [128, WIN], BF16)) for k in range(DT)]
                asb = [EG(nc.sbuf_tensor(f"g2{layer}_a{x}", [128, WIN], BF16)) for x in range(2)]
                gsb = [EG(nc.sbuf_tensor(f"g2{layer}_g{x}", [128, WIN], BF16)) for x in range(2)]
                gt = [EG(nc.sbuf_tensor(f"g2{layer}_gt{htx}", [HT_SIZES[htx], WIN], BF16))
                      for htx in range(NHT)]
                y2t_ = EG(nc.sbuf_tensor(f"g2{layer}_y2t", [128, D], F32))
                y2t = [y2t_, y2t_]

                pa = [EG(nc.psum_tensor(f"g2{layer}_pa{x}", [128, 512], F32)) for x in range(2)]
                pg = [EG(nc.psum_tensor(f"g2{layer}_pg{x}", [128, 512], F32)) for x in range(2)]
                pw = [EG(nc.psum_tensor(f"g2{layer}_pw{i}", [128, 512], F32)) for i in range(2)]
                ptb = EG(nc.psum_tensor(f"g2{layer}_ptb", [128, 512], BF16))
                pfx = EG(nc.psum_tensor(f"g2{layer}_pf", [128, 128], F32))

                def _wl(gp):
                    for s in (PE_, VE, SE_):
                        if s.n:
                            gp.wait_ge(s.h, s.n)
                    for k in range(DT):
                        Gq.inc(gp.dma_start(out=w1_sb[k][:, :],
                                            in_=w1_in[layer, k * 128:(k + 1) * 128, :]), 16)
                    for htx in range(NHT):
                        Gq.inc(gp.dma_start(out=w2_sb[htx][:, :],
                                            in_=w2_in[layer, HT_OFF[htx]:HT_OFF[htx] + HT_SIZES[htx], :]),
                               16)
                GB(_wl)
                g2_w_gq = Gq.n

                def _bz(v):
                    for s in (PE_, SE_):
                        if s.n:
                            v.wait_ge(s.h, s.n)
                    VE.inc(v.memset(btmp[:, :], 0.0))
                VB(_bz)

                def _bl(sy):
                    for s in (PE_, VE, SE_):
                        if s.n:
                            sy.wait_ge(s.h, s.n)
                    Lq.inc(sy.dma_start(out=btmp[0:21, :],
                                        in_=b1_in[layer, 0:2688].rearrange("(t p) -> t p", p=128)), 16)
                    Lq.inc(sy.dma_start(out=btmp[21:22, 0:42], in_=b1_in[layer, 2688:2730].unsqueeze(0)), 16)
                YB(_bl)
                bl_l = Lq.n

                def _bt(t):
                    t.wait_ge(Lq.h, bl_l)
                    PE_.inc(t.transpose(pfx[:, 0:22], btmp[0:22, :], idf[0:22, 0:22]))
                TB(_bt)

                def _btv(v):
                    v.wait_ge(PE_.h, PE_.n)
                    VE.inc(v.tensor_copy(b1aT[:, 0:22], pfx[:, 0:22]))
                VB(_btv)

                def _bl2(sy):
                    sy.wait_ge(VE.h, VE.n)
                    Lq.inc(sy.dma_start(out=btmp[0:21, :],
                                        in_=b1_in[layer, 2730:5418].rearrange("(t p) -> t p", p=128)), 16)
                    Lq.inc(sy.dma_start(out=btmp[21:22, 0:42], in_=b1_in[layer, 5418:5460].unsqueeze(0)), 16)
                YB(_bl2)
                bl2_l = Lq.n

                def _bt2(t):
                    t.wait_ge(Lq.h, bl2_l)
                    PE_.inc(t.transpose(pfx[:, 0:22], btmp[0:22, :], idf[0:22, 0:22]))
                TB(_bt2)

                def _btv2(v):
                    v.wait_ge(PE_.h, PE_.n)
                    VE.inc(v.tensor_copy(b1gT[:, 0:22], pfx[:, 0:22]))
                VB(_btv2)

                def _bl3(sy):
                    sy.wait_ge(VE.h, VE.n)
                    Lq.inc(sy.dma_start(out=b2row[:, :], in_=lg_in[layer:layer + 1, :]), 16)
                YB(_bl3)
                bl3_l = Lq.n

                def _bt3(t):
                    t.wait_ge(Lq.h, bl3_l)
                    PE_.inc(t.matmul(pw[0][:, :], onesf[0:1, :], b2row[0:1, 0:512],
                                     start=True, stop=True))
                    PE_.inc(t.matmul(pw[1][:, :], onesf[0:1, :], b2row[0:1, 512:1024],
                                     start=True, stop=True))
                TB(_bt3)

                def _btv3(v):
                    v.wait_ge(PE_.h, PE_.n)
                    VE.inc(v.tensor_copy(lg_rep[:, 0:512], pw[0][:, :]))
                    VE.inc(v.tensor_copy(lg_rep[:, 512:1024], pw[1][:, :]))
                VB(_btv3)

                def _bl4(sy):
                    sy.wait_ge(VE.h, VE.n)
                    Lq.inc(sy.dma_start(out=b2row[:, :], in_=lb_in[layer:layer + 1, :]), 16)
                YB(_bl4)
                bl4_l = Lq.n

                def _bt4(t):
                    t.wait_ge(Lq.h, bl4_l)
                    PE_.inc(t.matmul(pw[0][:, :], onesf[0:1, :], b2row[0:1, 0:512],
                                     start=True, stop=True))
                    PE_.inc(t.matmul(pw[1][:, :], onesf[0:1, :], b2row[0:1, 512:1024],
                                     start=True, stop=True))
                TB(_bt4)

                def _btv4(v):
                    v.wait_ge(PE_.h, PE_.n)
                    VE.inc(v.tensor_copy(lb_rep[:, 0:512], pw[0][:, :]))
                    VE.inc(v.tensor_copy(lb_rep[:, 512:1024], pw[1][:, :]))
                VB(_btv4)

                def _bl5(sy):
                    sy.wait_ge(VE.h, VE.n)
                    Lq.inc(sy.dma_start(out=b2row[:, :], in_=b2_in[layer:layer + 1, :]), 16)
                YB(_bl5)
                bl5_l = Lq.n

                def _bt5(t):
                    t.wait_ge(Lq.h, bl5_l)
                    PE_.inc(t.matmul(pw[0][:, :], onesf[0:1, :], b2row[0:1, 0:512],
                                     start=True, stop=True))
                    PE_.inc(t.matmul(pw[1][:, :], onesf[0:1, :], b2row[0:1, 512:1024],
                                     start=True, stop=True))
                TB(_bt5)

                def _btv5(v):
                    v.wait_ge(PE_.h, PE_.n)
                    VE.inc(v.tensor_copy(b2rep[:, 0:512], pw[0][:, :]))
                    VE.inc(v.tensor_copy(b2rep[:, 512:1024], pw[1][:, :]))
                VB(_btv5)
                init_ve = VE.n

                mu4 = EG(nc.sbuf_tensor(f"g2{layer}_mu4", [128, 4], F32))
                tm4 = EG(nc.sbuf_tensor(f"g2{layer}_tm4", [128, 4], F32))
                sa4 = EG(nc.sbuf_tensor(f"g2{layer}_sa4", [128, 4], F32))
                sb4 = EG(nc.sbuf_tensor(f"g2{layer}_sb4", [128, 4], F32))
                ve4 = EG(nc.sbuf_tensor(f"g2{layer}_ve4", [128, 4], F32))
                rstd4 = EG(nc.sbuf_tensor(f"g2{layer}_rstd4", [128, 4], F32))
                negb4 = EG(nc.sbuf_tensor(f"g2{layer}_negb4", [128, 4], F32))

                ptb_free = [init_ve]
                hid_pe = {}
                wres_ve = {}
                hev_se = []
                w2_used = {}
                for cc in range(8):
                    stat_se0 = SE_.n
                    for tt in range(4):
                        r = 1 + cc * WIN + tt * 128
                        ybi = tt % 2

                        def _yl(sy):
                            sy.wait_ge(VE.h, VE.n)
                            sy.wait_ge(SE_.h, SE_.n)
                            Yq.inc(sy.dma_start(out=y1f[ybi][:, :], in_=ybin[r:r + 128, :]), 16)
                            Yq.inc(sy.dma_start(out=y1s[ybi][:, :], in_=ybin[r - 1:r + 127, 512:1024]), 16)
                        YB(_yl)
                        yl_l = Yq.n

                        def _st1(v):
                            v.wait_ge(Yq.h, yl_l)
                            VE.inc(v.reduce_sum(mu4[:, tt:tt + 1], y1f[ybi][:, 0:512], axis=AX.X))
                            VE.inc(v.reduce_sum(tm4[:, tt:tt + 1], y1s[ybi][:, :], axis=AX.X))
                        VB(_st1)

                        def _st2(s):
                            s.wait_ge(Yq.h, yl_l)
                            s.wait_ge(SE_.h, SE_.n)
                            SE_.inc(s.activation(sqd[:, 0:512], y1f[ybi][:, 0:512], AF.Square,
                                                 accum_out=sa4[:, tt:tt + 1]))
                            SE_.inc(s.activation(sqd[:, 512:1024], y1s[ybi][:, :], AF.Square,
                                                 accum_out=sb4[:, tt:tt + 1]))
                        SB(_st2)

                    def _chain(v):
                        v.wait_ge(VE.h, VE.n)
                        v.wait_ge(SE_.h, SE_.n)
                        VE.inc(v.tensor_add(mu4[:, :], mu4[:, :], tm4[:, :]))
                        VE.inc(v.tensor_add(sa4[:, :], sa4[:, :], sb4[:, :]))
                        v.wait_ge(VE.h, VE.n)
                        VE.inc(v.tensor_scalar(mu4[:, :], mu4[:, :], 1.0 / D, None, ALU.mult))
                        VE.inc(v.tensor_scalar(ve4[:, :], sa4[:, :], 1.0 / D, EPS, ALU.mult, ALU.add))
                        v.wait_ge(VE.h, VE.n)
                        VE.inc(v.scalar_tensor_tensor(tm4[:, :], mu4[:, :], -1.0, mu4[:, :],
                                                      ALU.mult, ALU.mult))
                        v.wait_ge(VE.h, VE.n)
                        VE.inc(v.tensor_add(ve4[:, :], ve4[:, :], tm4[:, :]))
                    VB(_chain)

                    def _chain2v(v):
                        v.wait_ge(VE.h, VE.n)
                        VE.inc(v.reciprocal(rstd4[:, :], ve4[:, :]))
                    VB(_chain2v)

                    def _chain2(s):
                        s.wait_ge(VE.h, VE.n)
                        SE_.inc(s.activation(rstd4[:, :], rstd4[:, :], AF.Sqrt))
                    SB(_chain2)

                    def _chain3(v):
                        v.wait_ge(SE_.h, SE_.n)
                        VE.inc(v.scalar_tensor_tensor(negb4[:, :], mu4[:, :], -1.0, rstd4[:, :],
                                                      ALU.mult, ALU.mult))
                    VB(_chain3)
                    chain_ve = VE.n

                    for tt in range(4):
                        r = 1 + cc * WIN + tt * 128
                        ybi = tt % 2

                        def _yl2(sy):
                            sy.wait_ge(VE.h, VE.n)
                            sy.wait_ge(SE_.h, SE_.n)
                            Yq.inc(sy.dma_start(out=y1f[ybi][:, :], in_=ybin[r:r + 128, :]), 16)
                            Yq.inc(sy.dma_start(out=y1s[ybi][:, :], in_=ybin[r - 1:r + 127, 512:1024]), 16)
                        YB(_yl2)
                        yl2_l = Yq.n

                        def _nrm(s):
                            s.wait_ge(Yq.h, yl2_l)
                            s.wait_ge(VE.h, VE.n)
                            s.wait_ge(PE_.h, PE_.n)
                            s.wait_ge(VE.h, chain_ve)
                            SE_.inc(s.activation(hnb[:, 0:512], y1f[ybi][:, 0:512], AF.Identity,
                                                 bias=negb4[:, tt:tt + 1], scale=rstd4[:, tt:tt + 1]))
                            SE_.inc(s.activation(hnb[:, 512:1024], y1s[ybi][:, :], AF.Identity,
                                                 bias=negb4[:, tt:tt + 1], scale=rstd4[:, tt:tt + 1]))
                        SB(_nrm)
                        ln_se = SE_.n

                        def _aff(v):
                            v.wait_ge(SE_.h, ln_se)
                            VE.inc(v.tensor_mul(hnb[:, :], hnb[:, :], lg_rep[:, :]))
                            v.wait_ge(VE.h, VE.n)
                            VE.inc(v.tensor_add(hnb[:, :], hnb[:, :], lb_rep[:, :]))
                        VB(_aff)
                        aff_ve = VE.n

                        for g4 in range(2):
                            def _t1(t):
                                t.wait_ge(VE.h, aff_ve)
                                t.wait_ge(VE.h, ptb_free[0])
                                if cc - 1 in hid_pe and g4 == 0 and tt == 0:
                                    t.wait_ge(PE_.h, hid_pe[cc - 1])
                                for j in range(4):
                                    dk = g4 * 4 + j
                                    PE_.inc(t.transpose(ptb[:, j * 128:(j + 1) * 128],
                                                        hnb[:, dk * 128:(dk + 1) * 128], idb[:]))
                            TB(_t1)
                            t1_pe = PE_.n

                            def _e1(v):
                                v.wait_ge(PE_.h, t1_pe)
                                if cc - 1 in hid_pe and g4 == 0 and tt == 0:
                                    v.wait_ge(PE_.h, hid_pe[cc - 1])
                                for j in range(4):
                                    dk = g4 * 4 + j
                                    VE.inc(v.tensor_copy(hlnT[dk][:, tt * 128:(tt + 1) * 128],
                                                         ptb[:, j * 128:(j + 1) * 128]))
                                ptb_free[0] = VE.n
                            VB(_e1)

                    hln_ve = VE.n

                    for htx in range(NHT):
                        nP = HT_SIZES[htx]
                        a0 = HT_OFF[htx]
                        g0 = HID + HT_OFF[htx]
                        hpar = htx % 2

                        def _hmm(t):
                            if htx == 0:
                                t.wait_ge(Gq.h, g2_w_gq)
                                t.wait_ge(VE.h, hln_ve)
                            if len(hev_se) >= 2:
                                t.wait_ge(SE_.h, hev_se[-2])
                            for kt in range(DT):
                                PE_.inc(t.matmul(pa[hpar][0:nP, :], w1_sb[kt][:, a0:a0 + nP],
                                                 hlnT[kt][:, :], start=(kt == 0), stop=(kt == DT - 1)))
                            for kt in range(DT):
                                PE_.inc(t.matmul(pg[hpar][0:nP, :], w1_sb[kt][:, g0:g0 + nP],
                                                 hlnT[kt][:, :], start=(kt == 0), stop=(kt == DT - 1)))
                        TB(_hmm)
                        hmm_pe = PE_.n

                        def _hev(s):
                            s.wait_ge(PE_.h, hmm_pe)
                            if len(hev_se) >= 2:
                                s.wait_ge(VE.h, hmu_ve[-2])
                            SE_.inc(s.activation(asb[hpar][0:nP, :], pa[hpar][0:nP, :], AF.Identity,
                                                 bias=b1aT[0:nP, htx:htx + 1]))
                            SE_.inc(s.activation(gsb[hpar][0:nP, :], pg[hpar][0:nP, :], AF.Gelu,
                                                 bias=b1gT[0:nP, htx:htx + 1]))
                            hev_se.append(SE_.n)
                        SB(_hev)

                        def _hmu(v):
                            v.wait_ge(SE_.h, hev_se[-1])
                            if cc > 0:
                                v.wait_ge(PE_.h, w2_used[htx])
                            VE.inc(v.tensor_mul(gt[htx][0:nP, :], asb[hpar][0:nP, :],
                                                gsb[hpar][0:nP, :]))
                            hmu_ve.append(VE.n)
                        if cc == 0 and htx == 0:
                            hmu_ve = []
                        VB(_hmu)

                    gt_ve = VE.n
                    w2_used = {}

                    for tt in range(4):
                        ybi = tt % 2
                        rr = 1 + cc * WIN + tt * 128

                        def _ylr(sy):
                            sy.wait_ge(VE.h, VE.n)
                            sy.wait_ge(SE_.h, SE_.n)
                            Yq.inc(sy.dma_start(out=y1f[ybi][:, :], in_=ybin[rr:rr + 128, :]), 16)
                        YB(_ylr)
                        ylr_l = Yq.n

                        def _wmm(t):
                            if tt == 0:
                                t.wait_ge(VE.h, gt_ve)
                            else:
                                t.wait_ge(VE.h, VE.n)
                            for htx in range(NHT):
                                nP = HT_SIZES[htx]
                                for nb in range(2):
                                    PE_.inc(t.matmul(pw[nb][:, :],
                                                     gt[htx][0:nP, tt * 128:(tt + 1) * 128],
                                                     w2_sb[htx][0:nP, nb * 512:(nb + 1) * 512],
                                                     start=(htx == 0), stop=(htx == NHT - 1)))
                                w2_used[htx] = PE_.n
                        TB(_wmm)
                        wmm_pe = PE_.n

                        def _wres(v):
                            v.wait_ge(PE_.h, wmm_pe)
                            v.wait_ge(Yq.h, ylr_l)
                            v.wait_ge(Sq.h, Sq.n)
                            VE.inc(v.tensor_add(y2t[ybi][:, 0:512], y1f[ybi][:, 0:512], pw[0][:, :]))
                            VE.inc(v.tensor_add(y2t[ybi][:, 512:1024], y1f[ybi][:, 512:1024],
                                                pw[1][:, :]))
                            v.wait_ge(VE.h, VE.n)
                            VE.inc(v.tensor_add(y2t[ybi][:, :], y2t[ybi][:, :], b2rep[:, :]))
                        VB(_wres)

                        def _wst(sy):
                            sy.wait_ge(VE.h, VE.n)
                            ro = out_base + cc * WIN + tt * 128
                            Sq.inc(sy.dma_start(out=ybout[ro:ro + 128, :], in_=y2t[ybi][:, :]), 16)
                        YB(_wst)
                    hid_pe[cc] = PE_.n
                    wres_ve[cc] = VE.n

                g2_stores = Sq.n

                if tail_phase is not None:
                    xch_phase(tail_phase,
                              [(lambda s: tail_xch[s, :, :], ybout[out_base + T - 513:out_base + T, :])],
                              [(ybout[0:513, :], lambda s: tail_xch[s, :, :])],
                              wait_sem=Sq, wait_val=g2_stores)
                    if not comm:
                        def _zf2(v):
                            v.wait_ge(Sq.h, Sq.n)
                            VE.inc(v.memset(halot[:, :], 0.0))
                        VB(_zf2)

                        def _zs2(sy):
                            sy.wait_ge(VE.h, VE.n)
                            for r0, nr in [(0, 128), (128, 128), (256, 128), (384, 128), (512, 1)]:
                                Sq.inc(sy.dma_start(out=ybout[r0:r0 + nr, :], in_=halot[:nr, :]), 16)
                        YB(_zs2)
                    halo_tiles = [(0, 128), (128, 128), (256, 128), (384, 128), (512, 1)]
                    for r0, nr in halo_tiles:
                        def _hld(sy):
                            sy.wait_ge(XR.h, XR.n)
                            sy.wait_ge(VE.h, VE.n)
                            sy.wait_ge(Sq.h, Sq.n)
                            Yq.inc(sy.dma_start(out=halot[:nr, :], in_=ybout[r0:r0 + nr, :]), 16)
                        YB(_hld)
                        hl = Yq.n

                        def _hsc(v):
                            v.wait_ge(Yq.h, hl)
                            VE.inc(v.tensor_scalar(halot[:nr, :], halot[:nr, :], shsc[:nr, :],
                                                   None, ALU.mult))
                        VB(_hsc)

                        def _hst(sy):
                            sy.wait_ge(VE.h, VE.n)
                            Sq.inc(sy.dma_start(out=ybout[r0:r0 + nr, :], in_=halot[:nr, :]), 16)
                        YB(_hst)

            stage_barrier()

        if stages >= 2:
            run_g1(0, y0buf, y1buf, xch_y1, 2)
        if stages >= 3:
            run_g2(0, y1buf, y2buf, 3, xch_y2, 513)
        if stages >= 4:
            run_g1(1, y2buf, y3buf, xch_y3, 4)
        if stages >= 5:
            run_g2(1, y3buf, y4buf, None, None, 0)

        if stages >= 6:
          with contextlib.ExitStack() as sf:
            EF = sf.enter_context
            y4t = [EF(nc.sbuf_tensor(f"f_y4{i}", [128, D], F32)) for i in range(2)]
            xmt = [EF(nc.sbuf_tensor(f"f_xm{i}", [128, D], F32)) for i in range(2)]
            lnt = EF(nc.sbuf_tensor("f_ln", [128, D], F32))
            ot = EF(nc.sbuf_tensor("f_ot", [128, D], F32))

            for it in range(NT):
                bi = it % 2
                r0 = it * 128
                def _fl(sy):
                    sy.wait_ge(VE.h, VE.n)
                    sy.wait_ge(SE_.h, SE_.n)
                    Yq.inc(sy.dma_start(out=y4t[bi][:, :], in_=y4buf[r0:r0 + 128, :]), 16)
                    Yq.inc(sy.dma_start(out=xmt[bi][:, :], in_=xmidbuf[r0:r0 + 128, :]), 16)
                YB(_fl)
                fl_l = Yq.n

                emit_ln([y4t[bi][:, :]], [(lnt[:, :], y4t[bi][:, :])],
                        pre_wait=lambda e, fl=fl_l: e.wait_ge(Yq.h, fl))

                def _fa(v):
                    v.wait_ge(SE_.h, SE_.n)
                    v.wait_ge(Sq.h, Sq.n)
                    VE.inc(v.tensor_add(ot[:, :], xmt[bi][:, :], lnt[:, :]))
                VB(_fa)

                def _fs(sy):
                    sy.wait_ge(VE.h, VE.n)
                    Sq.inc(sy.dma_start(out=out_my[r0:r0 + 128, :], in_=ot[:, :]), 16)
                YB(_fs)

        if debug:
            REQ = {"dbg_xmid": 1, "dbg_y0": 1, "dbg_y1": 2, "dbg_y2": 3,
                   "dbg_y3": 4, "dbg_y4": 5}

            def _dbg(sy):
                sy.wait_ge(Sq.h, Sq.n)
                for nm, (oap, buf) in dbg.items():
                    if stages >= REQ[nm]:
                        Sq.inc(sy.dma_start(out=oap, in_=buf[:, :]), 16)
            YB(_dbg)

        def _fin(sy):
            sy.wait_ge(Sq.h, Sq.n)
            if Gq.n:
                sy.wait_ge(Gq.h, Gq.n)
            if XR.n:
                sy.wait_ge(XR.h, XR.n)
        YB(_fin)

    return nc


def _host_tables(h):
    t0 = h * T - WIN
    ts = np.arange(t0, t0 + TBL_ROWS, dtype=np.float64)
    half = np.arange(0, DH, 2, dtype=np.float64)
    inv_freq = 1.0 / 10000.0 ** (half / DH)
    f = ts[:, None] * inv_freq[None, :]
    freqs = np.concatenate([f, f], axis=1)
    base = (half + 0.4 * DH) / (1.4 * DH)
    power = (ts - N // 2) / float(WIN // 2)
    sc = base[None, :] ** power[:, None]
    scale = np.concatenate([sc, sc], axis=1)
    cos, sin = np.cos(freqs), np.sin(freqs)
    tqc = (cos * scale).astype(np.float32)
    tqs = (sin * scale).astype(np.float32)
    tkc = (cos / scale).astype(np.float32)
    tks = (sin / scale).astype(np.float32)
    if t0 < 0:
        nz = -t0
        for a in (tqc, tqs, tkc, tks):
            a[:nz] = 0.0
    def perm(a):
        return np.ascontiguousarray(
            a.reshape(TBL_T, 128, DH).transpose(1, 0, 2).reshape(128, TBL_T * DH))
    return perm(tqc), perm(tqs), perm(tkc), perm(tks)


def _host_masks(h):
    p = np.arange(128)[:, None]
    col = np.arange(1408)[None, :]
    d = col - 384
    big = np.where((d - p >= 0) & (d - p <= WIN), 0.0, NEG).astype(np.float32)
    m0 = np.zeros((8, 128, WIN), np.float32)
    q = np.arange(WIN)[None, :]
    for kt in range(8):
        if h == 0:
            if kt < 4:
                m0[kt, :, :] = NEG
            else:
                kg = (kt - 4) * 128 + np.arange(128)[:, None]
                m0[kt] = np.where(q - kg >= 0, 0.0, NEG)
        else:
            dd = q + 512 - kt * 128 - np.arange(128)[:, None]
            m0[kt] = np.where((dd >= 0) & (dd <= WIN), 0.0, NEG)
    return big.astype(ml_dtypes.bfloat16), m0.astype(ml_dtypes.bfloat16)


_NC_CACHE = {}


def _get_nc(debug=False, stages=99, comm=True, race=True):
    key = (debug, stages, comm, race)
    if key not in _NC_CACHE:
        _NC_CACHE[key] = build_nc(debug, stages, comm, race)
    return _NC_CACHE[key]


def make_in_maps(inputs, rng=None):
    x = np.ascontiguousarray(np.asarray(inputs["L0_attn_x"], dtype=np.float32))
    shared = {
        "wqkv": np.ascontiguousarray(np.asarray(inputs["L0_ffn_wqkv"], np.float32)),
        "wo": np.ascontiguousarray(np.asarray(inputs["L0_ffn_wo"], np.float32)),
        "w1": np.ascontiguousarray(np.asarray(inputs["L0_ffn_w1"], np.float32)),
        "w2": np.ascontiguousarray(np.asarray(inputs["L0_ffn_w2"], np.float32)),
        "qscale": np.ascontiguousarray(np.asarray(inputs["L0_ffn_q_scale"], np.float32)),
        "kscale": np.ascontiguousarray(np.asarray(inputs["L0_ffn_k_scale"], np.float32)),
        "ln_g": np.ascontiguousarray(np.asarray(inputs["L0_ffn_ln_g"], np.float32)),
        "ln_b": np.ascontiguousarray(np.asarray(inputs["L0_ffn_ln_b"], np.float32)),
        "b1": np.ascontiguousarray(np.asarray(inputs["L0_ffn_b1"], np.float32)),
        "b2": np.ascontiguousarray(np.asarray(inputs["L0_ffn_b2"], np.float32)),
        "idb": np.eye(128).astype(ml_dtypes.bfloat16),
        "idf": np.eye(128).astype(np.float32),
        "onesf": np.ones((1, 128), np.float32),
    }
    if rng is None:
        rng = np.random.default_rng()
    nonce = np.array([[int(rng.integers(1, 2**31 - 1))]], np.int32)
    in_maps = []
    for c in range(8):
        b, h = c // 2, c % 2
        tqc, tqs, tkc, tks = _host_tables(h)
        big, m0 = _host_masks(h)
        im = dict(shared)
        im["x_my"] = np.ascontiguousarray(x[b, h * T:(h + 1) * T])
        im["x_oth"] = np.ascontiguousarray(x[b, (1 - h) * T:(2 - h) * T])
        im["tqc"], im["tqs"], im["tkc"], im["tks"] = tqc, tqs, tkc, tks
        im["bigmask"], im["maskw0"] = big, m0
        im["nonce"] = nonce
        im["hscale"] = np.full((128, 1), float(h), np.float32)
        in_maps.append(im)
    return in_maps


def kernel(**inputs):
    nc = _get_nc(debug=False)
    in_maps = make_in_maps(inputs)
    res = run_bass_kernel_spmd(nc, in_maps, core_ids=list(range(8)))
    out = np.empty((B, N, D), np.float32)
    for c in range(8):
        b, h = c // 2, c % 2
        out[b, h * T:(h + 1) * T] = res.results[c]["out_my"]
    return out


# version 09 (final)
# speedup vs baseline: 1.2579x; 1.2579x over previous
import contextlib
import numpy as np
import ml_dtypes
import concourse.bass as bass
import concourse.mybir as mybir
from concourse.bass_utils import run_bass_kernel_spmd

F32 = mybir.dt.float32
BF16 = mybir.dt.bfloat16
I32 = mybir.dt.int32
AF = mybir.ActivationFunctionType
ALU = mybir.AluOpType
AX = mybir.AxisListType
EP = mybir.EngineType

B = 4
N = 8192
D = 1024
T = 4096
NT = 32
DT = 8
WIN = 512
HEADS = 8
DH = 64
INNER = 512
HID = 2730
HID2 = 5460
QKV = 1536
NEG = -3.0e38
EPS = 1e-5
NJT = 64
IG_SIZES = [3] * 10 + [2]
HT_SIZES = [128] * 21 + [42]
NHT = len(HT_SIZES)
HT_OFF = [0]
for _s in HT_SIZES[:-1]:
    HT_OFF.append(HT_OFF[-1] + _s)
TBL_ROWS = T + WIN
TBL_T = TBL_ROWS // 128


class S:
    def __init__(self, ctx, nc, name):
        self.h = ctx.enter_context(nc.semaphore(name))
        self.n = 0

    def inc(self, inst, k=1):
        inst.then_inc(self.h, k)
        self.n += k
        return inst


def build_nc(debug=False, stages=99, comm=True, race=True):
    nc = bass.Bass("TRN2", target_bir_lowering=False, detect_race_conditions=race)

    x_my = nc.dram_tensor("x_my", [T, D], F32, kind="ExternalInput").ap()
    x_oth = nc.dram_tensor("x_oth", [T, D], F32, kind="ExternalInput").ap()
    wqkv_in = nc.dram_tensor("wqkv", [2, D, QKV], F32, kind="ExternalInput").ap()
    wo_in = nc.dram_tensor("wo", [2, INNER, D], F32, kind="ExternalInput").ap()
    w1_in = nc.dram_tensor("w1", [2, D, HID2], F32, kind="ExternalInput").ap()
    w2_in = nc.dram_tensor("w2", [2, HID, D], F32, kind="ExternalInput").ap()
    qs_in = nc.dram_tensor("qscale", [2, DH], F32, kind="ExternalInput").ap()
    ks_in = nc.dram_tensor("kscale", [2, DH], F32, kind="ExternalInput").ap()
    lg_in = nc.dram_tensor("ln_g", [2, D], F32, kind="ExternalInput").ap()
    lb_in = nc.dram_tensor("ln_b", [2, D], F32, kind="ExternalInput").ap()
    b1_in = nc.dram_tensor("b1", [2, HID2], F32, kind="ExternalInput").ap()
    b2_in = nc.dram_tensor("b2", [2, D], F32, kind="ExternalInput").ap()
    tq_c = nc.dram_tensor("tqc", [128, TBL_T * DH], F32, kind="ExternalInput").ap()
    tq_s = nc.dram_tensor("tqs", [128, TBL_T * DH], F32, kind="ExternalInput").ap()
    tk_c = nc.dram_tensor("tkc", [128, TBL_T * DH], F32, kind="ExternalInput").ap()
    tk_s = nc.dram_tensor("tks", [128, TBL_T * DH], F32, kind="ExternalInput").ap()
    bigmask_in = nc.dram_tensor("bigmask", [128, 1408], BF16, kind="ExternalInput").ap()
    maskw0_in = nc.dram_tensor("maskw0", [8, 128, WIN], BF16, kind="ExternalInput").ap()
    idb_in = nc.dram_tensor("idb", [128, 128], BF16, kind="ExternalInput").ap()
    idf_in = nc.dram_tensor("idf", [128, 128], F32, kind="ExternalInput").ap()
    onesf_in = nc.dram_tensor("onesf", [1, 128], F32, kind="ExternalInput").ap()
    nonce_in = nc.dram_tensor("nonce", [1, 1], I32, kind="ExternalInput").ap()
    hs_in = nc.dram_tensor("hscale", [128, 1], F32, kind="ExternalInput").ap()

    out_my = nc.dram_tensor("out_my", [T, D], F32, kind="ExternalOutput").ap()

    lnbuf = nc.dram_tensor("lnbuf", [N, D], BF16).ap()
    sspill = nc.dram_tensor("sspill", [NJT, 8, 128, 512], BF16).ap()
    xmidbuf = nc.dram_tensor("xmidbuf", [T, D], F32).ap()
    y0buf = nc.dram_tensor("y0buf", [T + 513, D], F32).ap()
    y1buf = nc.dram_tensor("y1buf", [T + 1, D], F32).ap()
    y2buf = nc.dram_tensor("y2buf", [T + 513, D], F32).ap()
    y3buf = nc.dram_tensor("y3buf", [T + 1, D], F32).ap()
    y4buf = nc.dram_tensor("y4buf", [T, D], F32).ap()

    if debug:
        dbg = {}
        for nm, buf in [("dbg_xmid", xmidbuf), ("dbg_y0", y0buf), ("dbg_y1", y1buf),
                        ("dbg_y2", y2buf), ("dbg_y3", y3buf), ("dbg_y4", y4buf)]:
            dbg[nm] = (nc.dram_tensor(nm, list(buf.shape), F32, kind="ExternalOutput").ap(), buf)

    xch_stats = nc.dram_tensor("xch_stats", [8, 2, 128, NJT], F32, addr_space="Shared").ap()
    xch_xmid = nc.dram_tensor("xch_xmid", [8, 513, D], F32, addr_space="Shared").ap()
    xch_y1 = nc.dram_tensor("xch_y1", [8, 1, D], F32, addr_space="Shared").ap()
    xch_y2 = nc.dram_tensor("xch_y2", [8, 513, D], F32, addr_space="Shared").ap()
    xch_y3 = nc.dram_tensor("xch_y3", [8, 1, D], F32, addr_space="Shared").ap()
    xch_flag = nc.dram_tensor("xch_flag", [5, 8], I32, addr_space="Shared").ap()

    ctx = contextlib.ExitStack()
    with ctx:
        E = ctx.enter_context

        idb = E(nc.sbuf_tensor("s_idb", [128, 128], BF16))
        idf = E(nc.sbuf_tensor("s_idf", [128, 128], F32))
        onesf = E(nc.sbuf_tensor("s_onesf", [1, 128], F32))
        snonce = E(nc.sbuf_tensor("s_nonce", [1, 1], I32))
        sflag = E(nc.sbuf_tensor("s_flag", [1, 1], I32))
        shsc = E(nc.sbuf_tensor("s_hsc", [128, 1], F32))
        stM = E(nc.sbuf_tensor("s_stM", [128, NJT], F32))
        stZ = E(nc.sbuf_tensor("s_stZ", [128, NJT], F32))
        pM = E(nc.sbuf_tensor("s_pM", [128, NJT], F32))
        pZ = E(nc.sbuf_tensor("s_pZ", [128, NJT], F32))
        negMg = E(nc.sbuf_tensor("s_negMg", [128, NJT], F32))
        recipZ = E(nc.sbuf_tensor("s_recipZ", [128, NJT], F32))
        tmpw = E(nc.sbuf_tensor("s_tmpw", [128, NJT], F32))
        mu_t = E(nc.sbuf_tensor("s_mu", [128, 1], F32))
        tm_t = E(nc.sbuf_tensor("s_tm", [128, 1], F32))
        sa_t = E(nc.sbuf_tensor("s_sa", [128, 1], F32))
        sb_t = E(nc.sbuf_tensor("s_sb", [128, 1], F32))
        ve_t = E(nc.sbuf_tensor("s_ve", [128, 1], F32))
        rstd_t = E(nc.sbuf_tensor("s_rstd", [128, 1], F32))
        negb_t = E(nc.sbuf_tensor("s_negb", [128, 1], F32))
        sqd = E(nc.sbuf_tensor("s_sqd", [128, 1024], F32))
        halorow = E(nc.sbuf_tensor("s_halorow", [1, D], F32))
        halot = E(nc.sbuf_tensor("s_halot", [128, D], F32))

        Lq = S(ctx, nc, "Lq")
        Pq = S(ctx, nc, "Pq")
        LP = [S(ctx, nc, f"LP{i}") for i in range(2)]
        SR = [S(ctx, nc, f"SR{i}") for i in range(8)]
        SPq = [S(ctx, nc, f"SPq{i}") for i in range(2)]
        Tq = S(ctx, nc, "Tq")
        Tq2 = [S(ctx, nc, f"Tq2_{i}") for i in range(2)]
        HQ = [S(ctx, nc, f"HQ{i}") for i in range(2)]
        Yq = S(ctx, nc, "Yq")
        Sq = S(ctx, nc, "Sq")
        Gq = S(ctx, nc, "Gq")
        PE_ = S(ctx, nc, "PEm")
        VE = S(ctx, nc, "VEm")
        SE_ = S(ctx, nc, "SEm")
        GE = S(ctx, nc, "GEm")
        XR = S(ctx, nc, "XRm")
        PL = [S(ctx, nc, f"PL{i}") for i in range(5)]

        block = E(nc.Block())

        def YB(f):
            block.sync(f)

        def GB(f):
            block.gpsimd(f)

        def TB(f):
            block.tensor(f)

        def VB(f):
            block.vector(f)

        def SB(f):
            block.scalar(f)

        def stage_barrier():
            for mk in (YB, GB, TB, VB, SB):
                def _b(e):
                    for s in (PE_, VE, SE_, Lq, Sq, Gq, XR, Pq, Tq, Yq, *LP, *SR, *SPq,
                              *Tq2, *HQ):
                        if s.n:
                            e.wait_ge(s.h, s.n)
                mk(_b)

        def emit_ln(srcs, outs, pre_wait=None):
            P = srcs[0].shape[0]

            def _v1(v):
                v.wait_ge(SE_.h, SE_.n)
                if pre_wait is not None:
                    pre_wait(v)
                VE.inc(v.reduce_sum(mu_t[:P, :], srcs[0], axis=AX.X))
                if len(srcs) > 1:
                    VE.inc(v.reduce_sum(tm_t[:P, :], srcs[1], axis=AX.X))
                    v.wait_ge(VE.h, VE.n)
                    VE.inc(v.tensor_add(mu_t[:P, :], mu_t[:P, :], tm_t[:P, :]))
            VB(_v1)

            def _s1(s):
                if pre_wait is not None:
                    pre_wait(s)
                w0 = srcs[0].shape[-1]
                SE_.inc(s.activation(sqd[:P, 0:w0], srcs[0], AF.Square,
                                     accum_out=sa_t[:P, :]))
                if len(srcs) > 1:
                    w1 = srcs[1].shape[-1]
                    SE_.inc(s.activation(sqd[:P, w0:w0 + w1], srcs[1], AF.Square,
                                         accum_out=sb_t[:P, :]))
            SB(_s1)

            def _v2(v):
                v.wait_ge(SE_.h, SE_.n)
                v.wait_ge(VE.h, VE.n)
                if len(srcs) > 1:
                    VE.inc(v.tensor_add(sa_t[:P, :], sa_t[:P, :], sb_t[:P, :]))
                    v.wait_ge(VE.h, VE.n)
                nf = float(sum(s_.shape[-1] for s_ in srcs))
                VE.inc(v.tensor_scalar(mu_t[:P, :], mu_t[:P, :], 1.0 / nf, None, ALU.mult))
                VE.inc(v.tensor_scalar(ve_t[:P, :], sa_t[:P, :], 1.0 / nf, EPS, ALU.mult, ALU.add))
                v.wait_ge(VE.h, VE.n)
                VE.inc(v.scalar_tensor_tensor(tm_t[:P, :], mu_t[:P, :], -1.0, mu_t[:P, :],
                                              ALU.mult, ALU.mult))
                v.wait_ge(VE.h, VE.n)
                VE.inc(v.tensor_add(ve_t[:P, :], ve_t[:P, :], tm_t[:P, :]))
            VB(_v2)

            def _v2b(v):
                v.wait_ge(VE.h, VE.n)
                VE.inc(v.reciprocal(rstd_t[:P, :], ve_t[:P, :]))
            VB(_v2b)

            def _s2(s):
                s.wait_ge(VE.h, VE.n)
                SE_.inc(s.activation(rstd_t[:P, :], rstd_t[:P, :], AF.Sqrt))
            SB(_s2)

            def _v3(v):
                v.wait_ge(SE_.h, SE_.n)
                VE.inc(v.scalar_tensor_tensor(negb_t[:P, :], mu_t[:P, :], -1.0, rstd_t[:P, :],
                                              ALU.mult, ALU.mult))
            VB(_v3)

            def _s3(s):
                s.wait_ge(VE.h, VE.n)
                for o_ap, s_ap in outs:
                    SE_.inc(s.activation(o_ap, s_ap, AF.Identity,
                                         bias=negb_t[:P, :], scale=rstd_t[:P, :]))
            SB(_s3)

        def xch_phase(phase, write_pairs, read_pairs, wait_sem=None, wait_val=0,
                      extra_waits=()):
            if not comm:
                return
            gq_base = Gq.n

            def _g(gp):
                if wait_sem is not None and wait_val:
                    gp.wait_ge(wait_sem.h, wait_val)
                for sem_, val_ in extra_waits:
                    if val_:
                        gp.wait_ge(sem_.h, val_)
                gp.wait_ge(Pq.h, PRO_L)
                pid = nc.partition_id([EP.Pool])
                preg = gp.to_reg(gp.snap(pid))
                nd = len(write_pairs)
                for slot in range(8):
                    with gp.If_eq(preg, slot):
                        for fn, src in write_pairs:
                            gp.dma_start(out=fn(slot), in_=src).then_inc(Gq.h, 16)
                        gp.wait_ge(Gq.h, gq_base + nd * 16)
                        gp.dma_start(out=xch_flag[phase, slot:slot + 1].unsqueeze(0),
                                     in_=snonce[:]).then_inc(Gq.h, 16)
                Gq.n += (nd + 1) * 16
                nonce_reg = gp.alloc_register(f"xnon{phase}")
                gp.reg_load(nonce_reg, snonce[0:1, 0:1])
                flag_reg = gp.alloc_register(f"xflg{phase}")
                cond_reg = gp.alloc_register(f"xcnd{phase}")
                cnt_reg = gp.alloc_register(f"xcnt{phase}")
                gp.reg_mov(cond_reg, 1)
                gp.reg_mov(cnt_reg, 0)
                with gp.While(cond_reg):
                    gp.reg_add(cnt_reg, cnt_reg, 16)
                    for slot in range(8):
                        with gp.If_eq(preg, slot):
                            gp.dma_start(out=sflag[:],
                                         in_=xch_flag[phase, (slot ^ 1):(slot ^ 1) + 1].unsqueeze(0)
                                         ).then_inc(PL[phase].h, 16)
                    gp.wait_ge(PL[phase].h, cnt_reg)
                    gp.reg_load(flag_reg, sflag[0:1, 0:1])
                    gp.reg_alu(cond_reg, flag_reg, nonce_reg, ALU.not_equal)
                for slot in range(8):
                    with gp.If_eq(preg, slot):
                        for dst, fn in read_pairs:
                            gp.dma_start(out=dst, in_=fn(slot ^ 1)).then_inc(XR.h, 16)
                XR.n += len(read_pairs) * 16
            GB(_g)

        def _pro(sy):
            Pq.inc(sy.dma_start(out=idb[:], in_=idb_in), 16)
            Pq.inc(sy.dma_start(out=idf[:], in_=idf_in), 16)
            Pq.inc(sy.dma_start(out=onesf[:], in_=onesf_in), 16)
            Pq.inc(sy.dma_start(out=snonce[:], in_=nonce_in), 16)
            Pq.inc(sy.dma_start(out=shsc[:], in_=hs_in), 16)
        YB(_pro)
        PRO_L = Pq.n

        with contextlib.ExitStack() as spa:
            EA = spa.enter_context
            LNT = [EA(nc.sbuf_tensor(f"a_lnt{k}", [128, N], BF16)) for k in range(DT)]
            xld = [EA(nc.sbuf_tensor(f"p_xld{i}", [128, D], F32)) for i in range(4)]
            lnb = [EA(nc.sbuf_tensor(f"p_lnb{i}", [128, D], BF16)) for i in range(8)]
            muP = EA(nc.sbuf_tensor("p_mu", [128, 4], F32))
            saP = EA(nc.sbuf_tensor("p_sa", [128, 4], F32))
            veP = EA(nc.sbuf_tensor("p_ve", [128, 4], F32))
            tmP = EA(nc.sbuf_tensor("p_tm", [128, 4], F32))
            rsP = EA(nc.sbuf_tensor("p_rs", [128, 4], F32))
            nbP = EA(nc.sbuf_tensor("p_nb", [128, 4], F32))

            norm_done = []
            p1v_done = []
            st_done = []
            for grp in range(2 * NT // 4):
                gb = (grp % 2) * 4
                par = grp % 2

                def _ld(sy):
                    if grp >= 1:
                        sy.wait_ge(SE_.h, norm_done[grp - 1])
                        sy.wait_ge(VE.h, p1v_done[grp - 1])
                    for t in range(4):
                        g = grp * 4 + t
                        src = x_my if g < NT else x_oth
                        row0 = (g % NT) * 128
                        LP[par].inc(sy.dma_start(out=xld[t][:],
                                                 in_=src[row0:row0 + 128, :]), 16)
                YB(_ld)
                lw = LP[par].n

                def _p1v(v):
                    v.wait_ge(Pq.h, PRO_L)
                    v.wait_ge(LP[par].h, lw)
                    v.wait_ge(VE.h, VE.n)
                    for t in range(4):
                        VE.inc(v.reduce_sum(muP[:, t:t + 1], xld[t][:], axis=AX.X))
                VB(_p1v)
                p1v_done.append(VE.n)

                def _p1s(s):
                    s.wait_ge(Pq.h, PRO_L)
                    s.wait_ge(LP[par].h, lw)
                    for t in range(4):
                        s.wait_ge(SE_.h, SE_.n)
                        SE_.inc(s.activation(sqd[:, :], xld[t][:], AF.Square,
                                             accum_out=saP[:, t:t + 1]))
                SB(_p1s)

                def _ch(v):
                    v.wait_ge(VE.h, VE.n)
                    v.wait_ge(SE_.h, SE_.n)
                    VE.inc(v.tensor_scalar(muP[:, :], muP[:, :], 1.0 / D, None, ALU.mult))
                    VE.inc(v.tensor_scalar(veP[:, :], saP[:, :], 1.0 / D, EPS, ALU.mult, ALU.add))
                    v.wait_ge(VE.h, VE.n)
                    VE.inc(v.scalar_tensor_tensor(tmP[:, :], muP[:, :], -1.0, muP[:, :],
                                                  ALU.mult, ALU.mult))
                    v.wait_ge(VE.h, VE.n)
                    VE.inc(v.tensor_add(veP[:, :], veP[:, :], tmP[:, :]))
                    v.wait_ge(VE.h, VE.n)
                    VE.inc(v.reciprocal(rsP[:, :], veP[:, :]))
                VB(_ch)

                def _chs(s):
                    s.wait_ge(VE.h, VE.n)
                    SE_.inc(s.activation(rsP[:, :], rsP[:, :], AF.Sqrt))
                SB(_chs)

                def _ch3(v):
                    v.wait_ge(SE_.h, SE_.n)
                    VE.inc(v.scalar_tensor_tensor(nbP[:, :], muP[:, :], -1.0, rsP[:, :],
                                                  ALU.mult, ALU.mult))
                VB(_ch3)
                chP = VE.n

                def _p2(s):
                    s.wait_ge(VE.h, chP)
                    if grp >= 2:
                        s.wait_ge(SPq[par].h, st_done[grp - 2])
                    for t in range(4):
                        SE_.inc(s.activation(lnb[gb + t][:], xld[t][:], AF.Identity,
                                             bias=nbP[:, t:t + 1], scale=rsP[:, t:t + 1]))
                SB(_p2)
                norm_done.append(SE_.n)

                def _st(sy):
                    sy.wait_ge(SE_.h, SE_.n)
                    for t in range(4):
                        g = grp * 4 + t
                        row0 = (g % NT) * 128
                        dst = row0 if g < NT else T + row0
                        SPq[par].inc(sy.dma_start(out=lnbuf[dst:dst + 128, :],
                                                  in_=lnb[gb + t][:]), 16)
                    st_done.append(SPq[par].n)
                YB(_st)

            def _trld(sy):
                sy.wait_ge(SPq[0].h, SPq[0].n)
                sy.wait_ge(SPq[1].h, SPq[1].n)
                for k in range(DT):
                    Lq.inc(sy.dma_start(out=LNT[k][:, :], in_=lnbuf[:, k * 128:(k + 1) * 128],
                                        transpose=True), 16)
            YB(_trld)
            LNT_L = Lq.n

            with contextlib.ExitStack() as sa:
                EB = sa.enter_context
                srow = [EB(nc.sbuf_tensor(f"a_srow{i}", [128, T], BF16)) for i in range(2)]
                dmp = [EB(nc.sbuf_tensor(f"a_dmp{i}", [128, T], BF16)) for i in range(2)]
                negm_jt = EB(nc.sbuf_tensor("a_negm", [128, 1], F32))
                pA = [EB(nc.psum_tensor(f"a_pA{i}", [128, 512], F32)) for i in range(2)]

                a_ev = []
                spill_done = []
                stats_done = []

                for jt in range(NJT):
                    sbx = jt % 2

                    for ic in range(8):
                        idx = jt * 8 + ic
                        pb = ic % 2

                        def _mm(t):
                            if jt == 0 and ic == 0:
                                t.wait_ge(Lq.h, LNT_L)
                            if idx >= 2:
                                t.wait_ge(VE.h, a_ev[idx - 2])
                            for kt in range(DT):
                                PE_.inc(t.matmul(pA[pb][:, :],
                                                 LNT[kt][:, jt * 128:(jt + 1) * 128],
                                                 LNT[kt][:, ic * 512:(ic + 1) * 512],
                                                 start=(kt == 0), stop=(kt == DT - 1)))
                        TB(_mm)
                        mm_pe = PE_.n

                        def _ev(v):
                            if ic == 0 and jt >= 2:
                                v.wait_ge(SE_.h, stats_done[jt - 2])
                            v.wait_ge(PE_.h, mm_pe)
                            VE.inc(v.tensor_copy(srow[sbx][:, ic * 512:(ic + 1) * 512],
                                                 pA[pb][:, :]))
                            a_ev.append(VE.n)
                        VB(_ev)

                    def _stat(v):
                        v.wait_ge(VE.h, VE.n)
                        v.wait_ge(SE_.h, SE_.n)
                        VE.inc(v.reduce_max(stM[:, jt:jt + 1], srow[sbx][:], axis=AX.X))
                        v.wait_ge(VE.h, VE.n)
                        VE.inc(v.tensor_scalar(negm_jt[:], stM[:, jt:jt + 1], -1.0, None, ALU.mult))
                    VB(_stat)

                    def _exp(s):
                        s.wait_ge(VE.h, VE.n)
                        if jt >= 2:
                            s.wait_ge(SPq[jt % 2].h, spill_done[jt - 2])
                        SE_.inc(s.activation(dmp[sbx][:, :], srow[sbx][:], AF.Exp,
                                             bias=negm_jt[:], accum_out=stZ[:, jt:jt + 1]))
                        stats_done.append(SE_.n)
                    SB(_exp)

                    def _sp(sy):
                        sy.wait_ge(SE_.h, stats_done[-1])
                        for ic in range(8):
                            SPq[jt % 2].inc(sy.dma_start(
                                out=sspill[jt, ic, :, :],
                                in_=dmp[sbx][:, ic * 512:(ic + 1) * 512]), 16)
                        spill_done.append(SPq[jt % 2].n)
                    YB(_sp)

                def _stflush(sy):
                    sy.wait_ge(SE_.h, stats_done[-1])
                    sy.wait_ge(VE.h, VE.n)
                YB(_stflush)

                xch_phase(
                    0,
                    [(lambda s: xch_stats[s, 0, :, :], stM[:, :]),
                     (lambda s: xch_stats[s, 1, :, :], stZ[:, :])],
                    [(pM[:, 0:32], lambda s: xch_stats[s, 0, :, 32:64]),
                     (pM[:, 32:64], lambda s: xch_stats[s, 0, :, 0:32]),
                     (pZ[:, 0:32], lambda s: xch_stats[s, 1, :, 32:64]),
                     (pZ[:, 32:64], lambda s: xch_stats[s, 1, :, 0:32])],
                    wait_sem=SE_, wait_val=stats_done[-1],
                    extra_waits=[(VE, VE.n)])

                def _comb(v):
                    v.wait_ge(XR.h, XR.n)
                    v.wait_ge(SE_.h, SE_.n)
                    if not comm:
                        VE.inc(v.tensor_copy(pM[:, :], stM[:, :]))
                        VE.inc(v.tensor_copy(pZ[:, :], stZ[:, :]))
                        v.wait_ge(VE.h, VE.n)
                    VE.inc(v.tensor_max(negMg[:, :], stM[:, :], pM[:, :]))
                    v.wait_ge(VE.h, VE.n)
                    VE.inc(v.tensor_sub(tmpw[:, :], stM[:, :], negMg[:, :]))
                    VE.inc(v.tensor_sub(pM[:, :], pM[:, :], negMg[:, :]))
                VB(_comb)

                def _combs(s):
                    s.wait_ge(VE.h, VE.n)
                    SE_.inc(s.activation(tmpw[:, :], tmpw[:, :], AF.Exp))
                    SE_.inc(s.activation(pM[:, :], pM[:, :], AF.Exp))
                SB(_combs)

                def _comb2(v):
                    v.wait_ge(SE_.h, SE_.n)
                    VE.inc(v.tensor_mul(stZ[:, :], stZ[:, :], tmpw[:, :]))
                    VE.inc(v.tensor_mul(pZ[:, :], pZ[:, :], pM[:, :]))
                    v.wait_ge(VE.h, VE.n)
                    VE.inc(v.tensor_add(stZ[:, :], stZ[:, :], pZ[:, :]))
                    v.wait_ge(VE.h, VE.n)
                    VE.inc(v.reciprocal(recipZ[:, :], stZ[:, :]))
                    VE.inc(v.tensor_scalar(negMg[:, :], negMg[:, :], -1.0, None, ALU.mult))
                    v.wait_ge(VE.h, VE.n)
                    VE.inc(v.tensor_mul(recipZ[:, :], recipZ[:, :], tmpw[:, :]))
                VB(_comb2)

        stage_barrier()

        with contextlib.ExitStack() as sb:
            EB = sb.enter_context
            XT = [EB(nc.sbuf_tensor(f"b_xt{j}", [128, D], BF16)) for j in range(NJT)]
            ssb = [EB(nc.sbuf_tensor(f"b_ssb{i}", [128, 512], BF16)) for i in range(8)]
            ctx_sb = [EB(nc.sbuf_tensor(f"b_ctx{i}", [128, D], F32)) for i in range(4)]
            ctx_cp = []
            xres = EB(nc.sbuf_tensor("b_xres", [128, D], F32))
            ctxf = EB(nc.sbuf_tensor("b_ctxf", [128, D], F32))
            xmid_sb = EB(nc.sbuf_tensor("b_xmid", [128, D], F32))
            y0_sb = EB(nc.sbuf_tensor("b_y0", [128, D], F32))
            pO = [EB(nc.psum_tensor(f"b_pO{i}", [128, 1024], F32)) for i in range(4)]

            def _xt_ld(sy):
                for j in range(NJT):
                    Lq.inc(sy.dma_start(out=XT[j][:, :], in_=lnbuf[j * 128:(j + 1) * 128, :]), 16)
            YB(_xt_ld)
            xt_l = Lq.n

            def _xt_sc(v):
                v.wait_ge(Lq.h, xt_l)
                for j in range(NJT):
                    VE.inc(v.tensor_scalar(XT[j][:, :], XT[j][:, :], recipZ[:, j:j + 1],
                                           None, ALU.mult))
            VB(_xt_sc)
            xt_ve = VE.n

            sp_rd = []
            mm_ato = []
            for ig in range(8):
                for jt in range(NJT):
                    idx = ig * NJT + jt
                    sbx = idx % 8

                    def _srd(sy):
                        if idx >= 8:
                            sy.wait_ge(PE_.h, mm_ato[idx - 8])
                        SR[sbx].inc(sy.dma_start(out=ssb[sbx][:, :],
                                                  in_=sspill[jt, ig, :, :]), 16)
                        sp_rd.append(SR[sbx].n)
                    YB(_srd)

                    def _smm(t):
                        t.wait_ge(SR[sbx].h, sp_rd[idx])
                        if idx == 0:
                            t.wait_ge(VE.h, xt_ve)
                        if jt == 0 and ig > 0:
                            t.wait_ge(VE.h, ctx_cp[ig - 1])
                        for gsub in range(4):
                            for half in range(2):
                                PE_.inc(t.matmul(
                                    pO[gsub][:, half * 512:(half + 1) * 512],
                                    ssb[sbx][:, gsub * 128:(gsub + 1) * 128],
                                    XT[jt][:, half * 512:(half + 1) * 512],
                                    start=(jt == 0), stop=(jt == NJT - 1)))
                        mm_ato.append(PE_.n)
                    TB(_smm)

                grp_pe = mm_ato[-1]

                def _gcp(v):
                    v.wait_ge(PE_.h, grp_pe)
                    if ig > 0:
                        v.wait_ge(SE_.h, SE_.n)
                    for gsub in range(4):
                        VE.inc(v.tensor_copy(ctx_sb[gsub][:, :], pO[gsub][:, :]))
                VB(_gcp)
                ctx_cp.append(VE.n)

                for gsub in range(4):
                    it = ig * 4 + gsub
                    row0 = it * 128

                    def _xl(sy):
                        sy.wait_ge(VE.h, VE.n)
                        sy.wait_ge(SE_.h, SE_.n)
                        Lq.inc(sy.dma_start(out=xres[:], in_=x_my[row0:row0 + 128, :]), 16)
                    YB(_xl)
                    xl_l = Lq.n
                    gcp_ve = ctx_cp[-1]

                    emit_ln([ctx_sb[gsub][:, :]], [(ctxf[:], ctx_sb[gsub][:, :])],
                            pre_wait=lambda e: e.wait_ge(VE.h, gcp_ve))

                    sq_b = Sq.n

                    def _mid(v):
                        v.wait_ge(SE_.h, SE_.n)
                        v.wait_ge(Lq.h, xl_l)
                        v.wait_ge(Sq.h, sq_b)
                        VE.inc(v.tensor_add(xmid_sb[:], xres[:], ctxf[:]))
                    VB(_mid)
                    mid_ve = VE.n

                    def _y0gate(e):
                        e.wait_ge(VE.h, mid_ve)
                        e.wait_ge(Sq.h, sq_b)
                    emit_ln([xmid_sb[:]], [(y0_sb[:], xmid_sb[:])], pre_wait=_y0gate)

                    def _wr(sy):
                        sy.wait_ge(SE_.h, SE_.n)
                        sy.wait_ge(VE.h, VE.n)
                        Sq.inc(sy.dma_start(out=xmidbuf[row0:row0 + 128, :], in_=xmid_sb[:]), 16)
                        Sq.inc(sy.dma_start(out=y0buf[513 + row0:513 + row0 + 128, :],
                                            in_=y0_sb[:]), 16)
                    YB(_wr)

            xmid_stores = Sq.n

        stage_barrier()

        xch_phase(1,
                  [(lambda s: xch_xmid[s, :, :], xmidbuf[T - 513:T, :])],
                  [(y2buf[0:513, :], lambda s: xch_xmid[s, :, :])],
                  wait_sem=Sq, wait_val=xmid_stores)
        if not comm:
            def _zf(v):
                VE.inc(v.memset(halot[:, :], 0.0))
            VB(_zf)

            def _zs(sy):
                sy.wait_ge(VE.h, VE.n)
                for r0, nr in [(0, 128), (128, 128), (256, 128), (384, 128), (512, 1)]:
                    Sq.inc(sy.dma_start(out=y2buf[r0:r0 + nr, :], in_=halot[:nr, :]), 16)
            YB(_zs)

        with contextlib.ExitStack() as sh:
            halo_tiles = [(0, 128), (128, 128), (256, 128), (384, 128), (512, 1)]
            for r0, nr in halo_tiles:
                def _hld(sy):
                    sy.wait_ge(XR.h, XR.n)
                    sy.wait_ge(SE_.h, SE_.n)
                    sy.wait_ge(VE.h, VE.n)
                    sy.wait_ge(Sq.h, Sq.n)
                    Yq.inc(sy.dma_start(out=halot[:nr, :], in_=y2buf[r0:r0 + nr, :]), 16)
                YB(_hld)
                hl = Yq.n

                def _hsc(v):
                    v.wait_ge(Yq.h, hl)
                    v.wait_ge(SE_.h, SE_.n)
                    VE.inc(v.tensor_scalar(halot[:nr, :], halot[:nr, :], shsc[:nr, :],
                                           None, ALU.mult))
                VB(_hsc)
                hv = VE.n

                emit_ln([halot[:nr, :]], [(halot[:nr, :], halot[:nr, :])],
                        pre_wait=lambda e, hv=hv: e.wait_ge(VE.h, hv))

                def _hst(sy):
                    sy.wait_ge(SE_.h, SE_.n)
                    Sq.inc(sy.dma_start(out=y0buf[r0:r0 + nr, :], in_=halot[:nr, :]), 16)
                YB(_hst)

        stage_barrier()

        def run_g1(layer, ybin, ybout, xchrow, phase_row):
            with contextlib.ExitStack() as sg:
                EG = sg.enter_context
                wqkv_sb = [EG(nc.sbuf_tensor(f"g1{layer}_wqkv{k}", [128, QKV], BF16)) for k in range(DT)]
                wo_sb = [EG(nc.sbuf_tensor(f"g1{layer}_wo{c}", [128, D], BF16)) for c in range(4)]
                tqc_sb = [EG(nc.sbuf_tensor(f"g1{layer}_tqc{x}", [128, 4 * DH], F32)) for x in range(2)]
                tqs_sb = [EG(nc.sbuf_tensor(f"g1{layer}_tqs{x}", [128, 4 * DH], F32)) for x in range(2)]
                tkc_sb = [EG(nc.sbuf_tensor(f"g1{layer}_tkc{x}", [128, 4 * DH], F32)) for x in range(2)]
                tks_sb = [EG(nc.sbuf_tensor(f"g1{layer}_tks{x}", [128, 4 * DH], F32)) for x in range(2)]
                bigm_sb = EG(nc.sbuf_tensor(f"g1{layer}_bigm", [128, 1408], BF16))
                mw0_sb = EG(nc.sbuf_tensor(f"g1{layer}_mw0", [128, 8 * WIN], BF16))
                qsrow = EG(nc.sbuf_tensor(f"g1{layer}_qsrow", [1, DH], F32))
                ksrow = EG(nc.sbuf_tensor(f"g1{layer}_ksrow", [1, DH], F32))
                qs_rep = EG(nc.sbuf_tensor(f"g1{layer}_qsrep", [128, DH], F32))
                ks_rep = EG(nc.sbuf_tensor(f"g1{layer}_ksrep", [128, DH], F32))
                hsb = [[EG(nc.sbuf_tensor(f"g1{layer}_hsb{x}_{i}", [128, D], BF16))
                        for i in range(4)] for x in range(2)]
                hsT = [EG(nc.sbuf_tensor(f"g1{layer}_hsT{k}", [128, WIN], BF16)) for k in range(DT)]
                qf = [EG(nc.sbuf_tensor(f"g1{layer}_qf{x}", [128, INNER], F32)) for x in range(2)]
                kf = [EG(nc.sbuf_tensor(f"g1{layer}_kf{x}", [128, INNER], F32)) for x in range(2)]
                rot = [EG(nc.sbuf_tensor(f"g1{layer}_rot{x}", [128, INNER], F32)) for x in range(2)]
                t1b = [EG(nc.sbuf_tensor(f"g1{layer}_t1{x}", [128, INNER], F32)) for x in range(2)]
                nsb = [EG(nc.sbuf_tensor(f"g1{layer}_ns{x}", [128, HEADS], F32)) for x in range(2)]
                qrb = [EG(nc.sbuf_tensor(f"g1{layer}_qrb{x}", [128, INNER], BF16)) for x in range(2)]
                krb = [EG(nc.sbuf_tensor(f"g1{layer}_krb{x}", [128, INNER], BF16)) for x in range(2)]
                qT = [EG(nc.sbuf_tensor(f"g1{layer}_qT{p}", [128, WIN], BF16)) for p in range(4)]
                kT = [[EG(nc.sbuf_tensor(f"g1{layer}_kT{x}_{p}", [128, WIN], BF16)) for p in range(4)]
                      for x in range(2)]
                v65 = [[EG(nc.sbuf_tensor(f"g1{layer}_v65{x}_{tt}", [128, HEADS * 65], BF16))
                        for tt in range(4)] for x in range(2)]
                simT = [EG(nc.sbuf_tensor(f"g1{layer}_simT{i}", [128, 8 * WIN], F32)) for i in range(2)]
                PT = [EG(nc.sbuf_tensor(f"g1{layer}_PT{i}", [128, 8 * WIN], BF16)) for i in range(2)]
                kmax1 = EG(nc.sbuf_tensor(f"g1{layer}_kmax", [128, WIN], F32))
                kmax = [kmax1, kmax1]
                kmaxb = [EG(nc.sbuf_tensor(f"g1{layer}_kmaxb{i}", [128, WIN], BF16)) for i in range(2)]
                kmT1 = EG(nc.sbuf_tensor(f"g1{layer}_kmT", [128, WIN], BF16))
                kmT = [kmT1, kmT1]
                negq = [EG(nc.sbuf_tensor(f"g1{layer}_negq{i}", [128, 4], F32)) for i in range(2)]
                colrep1 = EG(nc.sbuf_tensor(f"g1{layer}_colrep", [128, WIN], BF16))
                colrep4 = [colrep1, colrep1]
                negrep1 = EG(nc.sbuf_tensor(f"g1{layer}_negrep", [128, WIN], BF16))
                negrep = [negrep1, negrep1]
                rz1 = EG(nc.sbuf_tensor(f"g1{layer}_rz", [128, 4], F32))
                rz4 = [rz1, rz1]
                av1 = EG(nc.sbuf_tensor(f"g1{layer}_av", [128, 256], BF16))
                av4 = [av1, av1]
                avT = [EG(nc.sbuf_tensor(f"g1{layer}_avT{c}", [128, WIN], BF16)) for c in range(4)]
                y0f = [EG(nc.sbuf_tensor(f"g1{layer}_y0f{i}", [128, D], F32)) for i in range(2)]
                y1t = [EG(nc.sbuf_tensor(f"g1{layer}_y1t{i}", [128, D], F32)) for i in range(2)]

                pq = [EG(nc.psum_tensor(f"g1{layer}_pq{i}", [128, 512], F32)) for i in range(2)]
                psim = [EG(nc.psum_tensor(f"g1{layer}_ps{i}", [128, 512], F32)) for i in range(4)]
                pav = EG(nc.psum_tensor(f"g1{layer}_pav", [128, 260], F32))
                ptb = EG(nc.psum_tensor(f"g1{layer}_ptb", [128, 512], BF16))

                def _wl(gp):
                    for s in (PE_, VE, SE_):
                        if s.n:
                            gp.wait_ge(s.h, s.n)
                    for k in range(DT):
                        Gq.inc(gp.dma_start(out=wqkv_sb[k][:, :],
                                            in_=wqkv_in[layer, k * 128:(k + 1) * 128, :]), 16)
                    for c in range(4):
                        Gq.inc(gp.dma_start(out=wo_sb[c][:, :],
                                            in_=wo_in[layer, c * 128:(c + 1) * 128, :]), 16)
                GB(_wl)

                def _tl(sy):
                    for s in (PE_, VE, SE_):
                        if s.n:
                            sy.wait_ge(s.h, s.n)
                    Lq.inc(sy.dma_start(out=bigm_sb[:], in_=bigmask_in), 16)
                    for kt in range(8):
                        Lq.inc(sy.dma_start(out=mw0_sb[:, kt * WIN:(kt + 1) * WIN],
                                            in_=maskw0_in[kt, :, :]), 16)
                    Lq.inc(sy.dma_start(out=qsrow[:], in_=qs_in[layer:layer + 1, :]), 16)
                    Lq.inc(sy.dma_start(out=ksrow[:], in_=ks_in[layer:layer + 1, :]), 16)
                YB(_tl)
                tl_l = Lq.n

                def _rep(t):
                    t.wait_ge(Lq.h, tl_l)
                    PE_.inc(t.matmul(pav[:, 0:DH], onesf[0:1, :], qsrow[0:1, :], start=True, stop=True))
                TB(_rep)

                def _repv(v):
                    v.wait_ge(PE_.h, PE_.n)
                    VE.inc(v.tensor_copy(qs_rep[:, :], pav[:, 0:DH]))
                VB(_repv)

                def _rep2(t):
                    t.wait_ge(VE.h, VE.n)
                    PE_.inc(t.matmul(pav[:, 65:65 + DH], onesf[0:1, :], ksrow[0:1, :],
                                     start=True, stop=True))
                TB(_rep2)

                def _rep2v(v):
                    v.wait_ge(PE_.h, PE_.n)
                    VE.inc(v.tensor_copy(ks_rep[:, :], pav[:, 65:65 + DH]))
                VB(_rep2v)
                fold_ve = VE.n

                def _vo(v):
                    for x in range(2):
                        for tt in range(4):
                            VE.inc(v.memset(v65[x][tt][:, :], 1.0))
                VB(_vo)
                g1_w_gq = Gq.n

                rot_done = {}
                htr_pe = {}
                res_ve = {}
                vev_se = {}
                qkv_pe = {}
                attn_pe = {}
                o_pe = {}
                avmm_pe = [0, 0]
                exp_se = [0, 0]
                ptb_free = [0]
                pav_free = [0]

                for cw in range(-1, 8):
                    cur = (cw + 1) % 2
                    prv = cw % 2
                    kv_only = (cw == -1)
                    base = 513 + cw * WIN

                    def _cl(gp):
                        gp.wait_ge(Gq.h, g1_w_gq)
                        if cw - 2 in htr_pe:
                            gp.wait_ge(PE_.h, htr_pe[cw - 2])
                        for tt in range(4):
                            r = base + tt * 128
                            HQ[cur].inc(gp.dma_start(out=hsb[cur][tt][:, 0:512],
                                                     in_=ybin[r:r + 128, 0:512]), 16)
                            HQ[cur].inc(gp.dma_start(out=hsb[cur][tt][:, 512:1024],
                                                     in_=ybin[r - 1:r + 127, 512:1024]), 16)
                    GB(_cl)
                    cl_g = HQ[cur].n

                    trow = (cw + 1) * 512
                    tpar = (cw + 1) % 2

                    def _ctl(sy):
                        if cw - 2 in rot_done:
                            sy.wait_ge(VE.h, rot_done[cw - 2])
                        tc0 = (cw + 1) * 4 * DH
                        for tin, tsb in ((tq_c, tqc_sb[tpar]), (tq_s, tqs_sb[tpar]),
                                         (tk_c, tkc_sb[tpar]), (tk_s, tks_sb[tpar])):
                            Tq2[tpar].inc(sy.dma_start(
                                out=tsb[:, :], in_=tin[:, tc0:tc0 + 4 * DH]), 16)
                    YB(_ctl)
                    ctl_l = Tq2[tpar].n

                    def _cfold(v):
                        v.wait_ge(Tq2[tpar].h, ctl_l)
                        if cw == -1:
                            v.wait_ge(VE.h, fold_ve)
                        for tc, srcrep in ((tqc_sb[tpar], qs_rep), (tkc_sb[tpar], ks_rep)):
                            t3 = tc[:].rearrange("p (t c) -> p t c", c=DH)
                            VE.inc(v.tensor_mul(t3, t3,
                                                srcrep[:].unsqueeze(1).broadcast_to((128, 4, DH))))
                        for ts, srcrep in ((tqs_sb[tpar], qs_rep), (tks_sb[tpar], ks_rep)):
                            t3 = ts[:].rearrange("p (t c) -> p t c", c=DH)
                            VE.inc(v.tensor_mul(t3[:, :, 0:32], t3[:, :, 0:32],
                                                srcrep[:, 32:64].unsqueeze(1).broadcast_to((128, 4, 32))))
                            VE.inc(v.tensor_mul(t3[:, :, 32:64], t3[:, :, 32:64],
                                                srcrep[:, 0:32].unsqueeze(1).broadcast_to((128, 4, 32))))
                    VB(_cfold)
                    fold_done = VE.n

                    hsT_ev = 0
                    for grp in range(8):
                        tt, dh4 = grp // 2, (grp % 2) * 4

                        def _tg(t):
                            if grp == 0:
                                t.wait_ge(HQ[cur].h, cl_g)
                                if cw == -1:
                                    t.wait_ge(Gq.h, g1_w_gq)
                            t.wait_ge(VE.h, ptb_free[0])
                            for j in range(4):
                                PE_.inc(t.transpose(ptb[:, j * 128:(j + 1) * 128],
                                                    hsb[cur][tt][:, (dh4 + j) * 128:(dh4 + j + 1) * 128],
                                                    idb[:]))
                        TB(_tg)
                        tg_pe = PE_.n

                        def _te(v):
                            if grp == 0 and cw - 1 in qkv_pe:
                                v.wait_ge(PE_.h, qkv_pe[cw - 1])
                            v.wait_ge(PE_.h, tg_pe)
                            for j in range(4):
                                VE.inc(v.tensor_copy(hsT[dh4 + j][:, tt * 128:(tt + 1) * 128],
                                                     ptb[:, j * 128:(j + 1) * 128]))
                            ptb_free[0] = VE.n
                        VB(_te)
                    htr_pe[cw] = PE_.n
                    hsT_ev = VE.n

                    qev_done = {}
                    kvtr_ve = 0
                    for tt in range(4):
                        tp2 = tt % 2
                        nb_list = [1, 2] if kv_only else [0, 1, 2]

                        qev_mid = [0]

                        def _qmm1(t):
                            t.wait_ge(VE.h, hsT_ev)
                            if tt == 0 and cw - 1 in res_ve:
                                t.wait_ge(VE.h, res_ve[cw - 1])
                            if tt > 0:
                                t.wait_ge(VE.h, qev_done[tt - 1][0])
                                t.wait_ge(SE_.h, qev_done[tt - 1][1])
                            if cw == -1 and tt == 0:
                                t.wait_ge(VE.h, fold_done)
                            if not kv_only:
                                for kt in range(DT):
                                    PE_.inc(t.matmul(pq[0][:, :],
                                                     hsT[kt][:, tt * 128:(tt + 1) * 128],
                                                     wqkv_sb[kt][:, 0:512],
                                                     start=(kt == 0), stop=(kt == DT - 1)))
                            for kt in range(DT):
                                PE_.inc(t.matmul(pq[1][:, :],
                                                 hsT[kt][:, tt * 128:(tt + 1) * 128],
                                                 wqkv_sb[kt][:, 512:1024],
                                                 start=(kt == 0), stop=(kt == DT - 1)))
                        TB(_qmm1)
                        qk_pe = PE_.n

                        def _qev(v):
                            v.wait_ge(PE_.h, qk_pe)
                            if not kv_only:
                                VE.inc(v.tensor_copy(qf[tp2][:, :], pq[0][:, :]))
                            VE.inc(v.tensor_copy(kf[tp2][:, :], pq[1][:, :]))
                            qev_mid[0] = VE.n
                        VB(_qev)

                        def _vmm(t):
                            t.wait_ge(VE.h, qev_mid[0])
                            for kt in range(DT):
                                PE_.inc(t.matmul(pq[0][:, :],
                                                 hsT[kt][:, tt * 128:(tt + 1) * 128],
                                                 wqkv_sb[kt][:, 1024:1536],
                                                 start=(kt == 0), stop=(kt == DT - 1)))
                        TB(_vmm)
                        v_pe = PE_.n

                        def _vev(s):
                            s.wait_ge(PE_.h, v_pe)
                            if tt == 0 and cw - 1 in attn_pe:
                                s.wait_ge(PE_.h, attn_pe[cw - 1])
                            SE_.inc(s.copy(
                                v65[cur][tt][:].rearrange("p (h s) -> p h s", s=65)[:, :, 0:DH],
                                pq[0][:].rearrange("p (h c) -> p h c", c=DH)))
                        SB(_vev)
                        vev_now = SE_.n

                        tcol = tt * DH

                        def _rot_pipeline(srcl, tc_sb, ts_sb, outb):
                            srcf = srcl[tp2]
                            def _r1(v):
                                v.wait_ge(VE.h, qev_mid[0])
                                src3 = srcf[:].rearrange("p (h c) -> p h c", c=DH)
                                VE.inc(v.tensor_mul(sqd[:, 0:INNER].rearrange("p (h c) -> p h c", c=DH),
                                                    src3, src3))
                                v.wait_ge(VE.h, VE.n)
                                VE.inc(v.reduce_sum(nsb[tp2][:].unsqueeze(2),
                                                    sqd[:, 0:INNER].rearrange("p (h c) -> p h c", c=DH),
                                                    axis=AX.X))
                                v.wait_ge(VE.h, VE.n)
                                VE.inc(v.tensor_scalar_max(nsb[tp2][:], nsb[tp2][:], 1e-24))
                            VB(_r1)

                            def _r2v(v):
                                v.wait_ge(VE.h, VE.n)
                                VE.inc(v.reciprocal(nsb[tp2][:], nsb[tp2][:]))
                            VB(_r2v)

                            def _r2(s):
                                s.wait_ge(VE.h, VE.n)
                                SE_.inc(s.activation(nsb[tp2][:], nsb[tp2][:], AF.Sqrt))
                            SB(_r2)

                            def _r3(v):
                                v.wait_ge(SE_.h, SE_.n)
                                src3 = srcf[:].rearrange("p (h c) -> p h c", c=DH)
                                VE.inc(v.tensor_mul(src3, src3,
                                                    nsb[tp2][:].unsqueeze(2).broadcast_to((128, HEADS, DH))))
                                v.wait_ge(VE.h, VE.n)
                                rot3 = rot[tp2][:].rearrange("p (h c) -> p h c", c=DH)
                                VE.inc(v.tensor_scalar(rot3[:, :, 0:32], src3[:, :, 32:64],
                                                       -1.0, None, ALU.mult))
                                VE.inc(v.tensor_copy(rot3[:, :, 32:64], src3[:, :, 0:32]))
                                tc3 = tc_sb[:, tcol:tcol + DH].unsqueeze(1).broadcast_to((128, HEADS, DH))
                                ts3 = ts_sb[:, tcol:tcol + DH].unsqueeze(1).broadcast_to((128, HEADS, DH))
                                t13 = t1b[tp2][:].rearrange("p (h c) -> p h c", c=DH)
                                VE.inc(v.tensor_mul(t13, src3, tc3))
                                v.wait_ge(VE.h, VE.n)
                                VE.inc(v.tensor_mul(rot3, rot3, ts3))
                                v.wait_ge(VE.h, VE.n)
                                VE.inc(v.tensor_tensor(outb[tp2][:].rearrange("p (h c) -> p h c", c=DH),
                                                       t13, rot3, ALU.add))
                            VB(_r3)

                        _rot_pipeline(kf, tkc_sb[tpar], tks_sb[tpar], krb)
                        if not kv_only:
                            _rot_pipeline(qf, tqc_sb[tpar], tqs_sb[tpar], qrb)
                        rotk_ve = VE.n

                        def _tk(t):
                            t.wait_ge(VE.h, rotk_ve)
                            t.wait_ge(VE.h, ptb_free[0])
                            for hp in range(4):
                                PE_.inc(t.transpose(ptb[:, hp * 128:(hp + 1) * 128],
                                                    krb[tp2][:, hp * 128:(hp + 1) * 128], idb[:]))
                        TB(_tk)
                        tk_pe = PE_.n

                        def _ek(v):
                            if tt == 0 and cw - 1 in attn_pe:
                                v.wait_ge(PE_.h, attn_pe[cw - 1])
                            v.wait_ge(PE_.h, tk_pe)
                            for hp in range(4):
                                VE.inc(v.tensor_copy(kT[cur][hp][:, tt * 128:(tt + 1) * 128],
                                                     ptb[:, hp * 128:(hp + 1) * 128]))
                            ptb_free[0] = VE.n
                        VB(_ek)

                        if not kv_only:
                            def _tq(t):
                                t.wait_ge(VE.h, ptb_free[0])
                                for hp in range(4):
                                    PE_.inc(t.transpose(ptb[:, hp * 128:(hp + 1) * 128],
                                                        qrb[tp2][:, hp * 128:(hp + 1) * 128], idb[:]))
                            TB(_tq)
                            tq_pe = PE_.n

                            def _eq(v):
                                v.wait_ge(PE_.h, tq_pe)
                                for hp in range(4):
                                    VE.inc(v.tensor_copy(qT[hp][:, tt * 128:(tt + 1) * 128],
                                                         ptb[:, hp * 128:(hp + 1) * 128]))
                                ptb_free[0] = VE.n
                            VB(_eq)

                        qev_done[tt] = (VE.n, vev_now)

                    rot_done[cw] = VE.n
                    vev_se[cw] = SE_.n
                    qkv_pe[cw] = PE_.n
                    kv_ready = VE.n

                    if kv_only:
                        attn_pe[cw] = PE_.n
                        o_pe[cw] = PE_.n
                        continue

                    sim_ev = []

                    for hp in range(4):
                        for kt in range(8):
                            ksrc = kT[prv][hp] if kt < 4 else kT[cur][hp]
                            kcol = (kt % 4) * 128
                            bk = (len(sim_ev) // 2) % 2

                            def _smm(t):
                                if not sim_ev:
                                    t.wait_ge(VE.h, kv_ready)
                                if len(sim_ev) >= 4:
                                    t.wait_ge(VE.h, sim_ev[-3])
                                PE_.inc(t.matmul(psim[2 * bk][:, :], ksrc[0:64, kcol:kcol + 128],
                                                 qT[hp][0:64, :], start=True, stop=True,
                                                 tile_position=(0, 0)))
                                PE_.inc(t.matmul(psim[2 * bk + 1][:, :], ksrc[64:128, kcol:kcol + 128],
                                                 qT[hp][64:128, :], start=True, stop=True,
                                                 tile_position=(64, 0)))
                            TB(_smm)
                            smm_pe = PE_.n

                            def _sev(v):
                                if kt == 0:
                                    v.wait_ge(SE_.h, max(exp_se))
                                v.wait_ge(PE_.h, smm_pe)
                                for sdx in range(2):
                                    if cw == 0:
                                        msk = mw0_sb[:, kt * WIN:(kt + 1) * WIN]
                                    else:
                                        c0 = 896 - kt * 128
                                        msk = bigm_sb[:, c0:c0 + WIN]
                                    VE.inc(v.scalar_tensor_tensor(
                                        simT[sdx][:, kt * WIN:(kt + 1) * WIN],
                                        psim[2 * bk + sdx][:, :], 8.0, msk, ALU.mult, ALU.add))
                                    sim_ev.append(VE.n)
                            VB(_sev)

                        pair_ev = sim_ev[-1]

                        def _mx2(v):
                            v.wait_ge(VE.h, pair_ev)
                            for sdx in range(2):
                                VE.inc(v.reduce_max(
                                    kmax[sdx][:].unsqueeze(2),
                                    simT[sdx][:].rearrange("p (k q) -> p q k", k=8),
                                    axis=AX.X))
                                v.wait_ge(VE.h, VE.n)
                                VE.inc(v.tensor_copy(kmaxb[sdx][:, :], kmax[sdx][:, :]))
                                v.wait_ge(VE.h, VE.n)
                        VB(_mx2)
                        mx_ve = VE.n

                        rmx_ve = [0, 0]
                        for sdx in range(2):
                            def _trm(t):
                                t.wait_ge(VE.h, mx_ve)
                                t.wait_ge(VE.h, ptb_free[0])
                                for qt in range(4):
                                    PE_.inc(t.transpose(ptb[:, qt * 128:(qt + 1) * 128],
                                                        kmaxb[sdx][:, qt * 128:(qt + 1) * 128], idb[:]))
                            TB(_trm)
                            trm_pe = PE_.n

                            def _rmx(v):
                                v.wait_ge(PE_.h, trm_pe)
                                v.wait_ge(VE.h, VE.n)
                                VE.inc(v.tensor_copy(kmT[sdx][:, :], ptb[:, :]))
                                ptb_free[0] = VE.n
                                v.wait_ge(VE.h, VE.n)
                                VE.inc(v.reduce_max(negq[sdx][:].unsqueeze(2),
                                                    kmT[sdx][:].rearrange("p (t j) -> p t j", t=4),
                                                    axis=AX.X, negate=True))
                            VB(_rmx)
                            rmx_ve[sdx] = VE.n

                        sub_ve = [0, 0]
                        sub_g = [0]
                        pt_se2 = [0, 0]
                        trc_pe2 = [0, 0]
                        for sdx in range(2):
                            def _crep(v):
                                v.wait_ge(VE.h, rmx_ve[sdx])
                                if sdx == 1:
                                    v.wait_ge(PE_.h, trc_pe2[0])
                                for qt in range(4):
                                    VE.inc(v.tensor_scalar(colrep4[sdx][:, qt * 128:(qt + 1) * 128],
                                                           kmaxb[sdx][:, 0:128], 0.0,
                                                           negq[sdx][:, qt:qt + 1],
                                                           ALU.mult, ALU.add))
                            VB(_crep)
                            crep_ve = VE.n

                            def _trc(t):
                                t.wait_ge(VE.h, crep_ve)
                                t.wait_ge(VE.h, ptb_free[0])
                                for qt in range(4):
                                    PE_.inc(t.transpose(ptb[:, qt * 128:(qt + 1) * 128],
                                                        colrep4[sdx][:, qt * 128:(qt + 1) * 128], idb[:]))
                            TB(_trc)
                            trc_pe = PE_.n
                            trc_pe2[sdx] = PE_.n

                            def _erc(v):
                                v.wait_ge(PE_.h, trc_pe)
                                v.wait_ge(VE.h, VE.n)
                                VE.inc(v.tensor_copy(negrep[sdx][:, :], ptb[:, :]))
                                ptb_free[0] = VE.n
                                v.wait_ge(VE.h, VE.n)
                                VE.inc(v.tensor_tensor(
                                    simT[sdx][:].rearrange("p (k q) -> p k q", k=8),
                                    simT[sdx][:].rearrange("p (k q) -> p k q", k=8),
                                    negrep[sdx][:].unsqueeze(1).broadcast_to((128, 8, WIN)),
                                    ALU.add))
                            VB(_erc)
                            sub_ve[sdx] = VE.n

                            def _exp(s):
                                s.wait_ge(VE.h, sub_ve[sdx])
                                s.wait_ge(PE_.h, avmm_pe[sdx])
                                SE_.inc(s.activation(PT[sdx][:, :], simT[sdx][:, :], AF.Exp))
                            SB(_exp)
                            pt_se2[sdx] = SE_.n
                            exp_se[sdx] = SE_.n

                        avz_ve = [0, 0]
                        for sdx in range(2):
                            h = hp * 2 + sdx

                            def _avm(t):
                                t.wait_ge(SE_.h, pt_se2[sdx])
                                t.wait_ge(VE.h, pav_free[0])
                                for qt in range(4):
                                    for kt in range(8):
                                        vsrc = v65[prv][kt % 4] if kt < 4 else v65[cur][kt % 4]
                                        PE_.inc(t.matmul(
                                            pav[:, qt * 65:(qt + 1) * 65],
                                            PT[sdx][:, kt * WIN + qt * 128: kt * WIN + (qt + 1) * 128],
                                            vsrc[:, h * 65:(h + 1) * 65],
                                            start=(kt == 0), stop=(kt == 7)))
                            TB(_avm)
                            avmm_pe[sdx] = PE_.n

                            def _avz(v):
                                v.wait_ge(PE_.h, avmm_pe[sdx])
                                v.wait_ge(VE.h, VE.n)
                                pv = pav[:].rearrange("p (q s) -> p q s", s=65)
                                VE.inc(v.reciprocal(rz4[sdx][:].unsqueeze(2), pv[:, :, 64:65]))
                                v.wait_ge(VE.h, VE.n)
                                VE.inc(v.tensor_tensor(av4[sdx][:].rearrange("p (q c) -> p q c", c=DH),
                                                       pv[:, :, 0:DH],
                                                       rz4[sdx][:].unsqueeze(2).broadcast_to((128, 4, DH)),
                                                       ALU.mult))
                                pav_free[0] = VE.n
                            VB(_avz)
                            avz_ve[sdx] = VE.n

                            def _avt(t):
                                t.wait_ge(VE.h, avz_ve[sdx])
                                t.wait_ge(VE.h, ptb_free[0])
                                for qt in range(4):
                                    PE_.inc(t.transpose(ptb[0:DH, qt * 128:(qt + 1) * 128],
                                                        av4[sdx][:, qt * DH:(qt + 1) * DH], idb[:]))
                            TB(_avt)
                            avt_pe = PE_.n

                            def _ave(v):
                                if hp == 0 and sdx == 0 and cw - 1 in o_pe:
                                    v.wait_ge(PE_.h, o_pe[cw - 1])
                                v.wait_ge(PE_.h, avt_pe)
                                VE.inc(v.tensor_copy(avT[hp][sdx * DH:(sdx + 1) * DH, :],
                                                     ptb[0:DH, :]))
                                ptb_free[0] = VE.n
                            VB(_ave)

                    attn_pe[cw] = PE_.n
                    avT_ve = VE.n

                    for tt in range(4):
                        r = base + tt * 128
                        ybi = tt % 2

                        def _ylf(sy):
                            if cw - 1 in res_ve:
                                sy.wait_ge(VE.h, res_ve[cw - 1])
                            if tt >= 2:
                                sy.wait_ge(VE.h, VE.n)
                            Yq.inc(sy.dma_start(out=y0f[ybi][:, :], in_=ybin[r:r + 128, :]), 16)
                        YB(_ylf)
                        ylf_l = Yq.n

                        def _omm(t):
                            t.wait_ge(VE.h, avT_ve if tt == 0 else VE.n)
                            for ct in range(4):
                                for nb in range(2):
                                    PE_.inc(t.matmul(pq[nb][:, :],
                                                     avT[ct][:, tt * 128:(tt + 1) * 128],
                                                     wo_sb[ct][:, nb * 512:(nb + 1) * 512],
                                                     start=(ct == 0), stop=(ct == 3)))
                        TB(_omm)
                        omm_pe = PE_.n

                        def _res(v):
                            v.wait_ge(PE_.h, omm_pe)
                            v.wait_ge(Yq.h, ylf_l)
                            v.wait_ge(Sq.h, Sq.n)
                            VE.inc(v.tensor_add(y1t[ybi][:, 0:512], y0f[ybi][:, 0:512], pq[0][:, :]))
                            VE.inc(v.tensor_add(y1t[ybi][:, 512:1024], y0f[ybi][:, 512:1024],
                                                pq[1][:, :]))
                        VB(_res)

                        def _yst(sy):
                            sy.wait_ge(VE.h, VE.n)
                            ro = 1 + cw * WIN + tt * 128
                            Sq.inc(sy.dma_start(out=ybout[ro:ro + 128, :], in_=y1t[ybi][:, :]), 16)
                        YB(_yst)
                    res_ve[cw] = VE.n
                    o_pe[cw] = PE_.n

                g1_stores = Sq.n
                xch_phase(phase_row,
                          [(lambda s: xchrow[s, :, :], ybout[T:T + 1, :])],
                          [(halorow[:, :], lambda s: xchrow[s, :, :])],
                          wait_sem=Sq, wait_val=g1_stores)

                def _hrow(v):
                    v.wait_ge(XR.h, XR.n)
                    if not comm:
                        VE.inc(v.memset(halorow[:, :], 0.0))
                        v.wait_ge(VE.h, VE.n)
                    VE.inc(v.tensor_scalar(halorow[:, :], halorow[:, :], shsc[0:1, :], None,
                                           ALU.mult))
                VB(_hrow)

                def _hrst(sy):
                    sy.wait_ge(VE.h, VE.n)
                    Sq.inc(sy.dma_start(out=ybout[0:1, :], in_=halorow[:, :]), 16)
                YB(_hrst)

            stage_barrier()

        def run_g2(layer, ybin, ybout, tail_phase, tail_xch, out_base):
            with contextlib.ExitStack() as sg:
                EG = sg.enter_context
                w1_sb = [EG(nc.sbuf_tensor(f"g2{layer}_w1{k}", [128, HID2], BF16)) for k in range(DT)]
                w2_sb = [EG(nc.sbuf_tensor(f"g2{layer}_w2{htx}", [HT_SIZES[htx], D], BF16))
                         for htx in range(NHT)]
                b1aT = EG(nc.sbuf_tensor(f"g2{layer}_b1aT", [128, NHT], F32))
                b1gT = EG(nc.sbuf_tensor(f"g2{layer}_b1gT", [128, NHT], F32))
                lg_rep = EG(nc.sbuf_tensor(f"g2{layer}_lgrep", [128, D], BF16))
                lb_rep = EG(nc.sbuf_tensor(f"g2{layer}_lbrep", [128, D], BF16))
                b2rep = EG(nc.sbuf_tensor(f"g2{layer}_b2rep", [128, D], F32))
                btmp = EG(nc.sbuf_tensor(f"g2{layer}_btmp", [22, 128], F32))
                b2row = halot[0:1, :]
                y1f = [EG(nc.sbuf_tensor(f"g2{layer}_y1f{i}", [128, D], F32)) for i in range(2)]
                y1s_ = EG(nc.sbuf_tensor(f"g2{layer}_y1s", [128, 512], F32))
                y1s = [y1s_, y1s_]
                hnb = EG(nc.sbuf_tensor(f"g2{layer}_hnb", [128, D], BF16))
                hlnT = [EG(nc.sbuf_tensor(f"g2{layer}_hlnT{k}", [128, WIN], BF16)) for k in range(DT)]
                asb = [EG(nc.sbuf_tensor(f"g2{layer}_a{x}", [128, WIN], BF16)) for x in range(2)]
                gsb = [EG(nc.sbuf_tensor(f"g2{layer}_g{x}", [128, WIN], BF16)) for x in range(2)]
                gt = [EG(nc.sbuf_tensor(f"g2{layer}_gt{htx}", [HT_SIZES[htx], WIN], BF16))
                      for htx in range(NHT)]
                y2t_ = EG(nc.sbuf_tensor(f"g2{layer}_y2t", [128, D], F32))
                y2t = [y2t_, y2t_]

                pa = [EG(nc.psum_tensor(f"g2{layer}_pa{x}", [128, 512], F32)) for x in range(2)]
                pg = [EG(nc.psum_tensor(f"g2{layer}_pg{x}", [128, 512], F32)) for x in range(2)]
                pw = [EG(nc.psum_tensor(f"g2{layer}_pw{i}", [128, 512], F32)) for i in range(2)]
                ptb = EG(nc.psum_tensor(f"g2{layer}_ptb", [128, 512], BF16))
                pfx = EG(nc.psum_tensor(f"g2{layer}_pf", [128, 128], F32))

                def _wl(gp):
                    for s in (PE_, VE, SE_):
                        if s.n:
                            gp.wait_ge(s.h, s.n)
                    for k in range(DT):
                        Gq.inc(gp.dma_start(out=w1_sb[k][:, :],
                                            in_=w1_in[layer, k * 128:(k + 1) * 128, :]), 16)
                    for htx in range(NHT):
                        Gq.inc(gp.dma_start(out=w2_sb[htx][:, :],
                                            in_=w2_in[layer, HT_OFF[htx]:HT_OFF[htx] + HT_SIZES[htx], :]),
                               16)
                GB(_wl)
                g2_w_gq = Gq.n

                def _bz(v):
                    for s in (PE_, SE_):
                        if s.n:
                            v.wait_ge(s.h, s.n)
                    VE.inc(v.memset(btmp[:, :], 0.0))
                VB(_bz)

                def _bl(sy):
                    for s in (PE_, VE, SE_):
                        if s.n:
                            sy.wait_ge(s.h, s.n)
                    Lq.inc(sy.dma_start(out=btmp[0:21, :],
                                        in_=b1_in[layer, 0:2688].rearrange("(t p) -> t p", p=128)), 16)
                    Lq.inc(sy.dma_start(out=btmp[21:22, 0:42], in_=b1_in[layer, 2688:2730].unsqueeze(0)), 16)
                YB(_bl)
                bl_l = Lq.n

                def _bt(t):
                    t.wait_ge(Lq.h, bl_l)
                    PE_.inc(t.transpose(pfx[:, 0:22], btmp[0:22, :], idf[0:22, 0:22]))
                TB(_bt)

                def _btv(v):
                    v.wait_ge(PE_.h, PE_.n)
                    VE.inc(v.tensor_copy(b1aT[:, 0:22], pfx[:, 0:22]))
                VB(_btv)

                def _bl2(sy):
                    sy.wait_ge(VE.h, VE.n)
                    Lq.inc(sy.dma_start(out=btmp[0:21, :],
                                        in_=b1_in[layer, 2730:5418].rearrange("(t p) -> t p", p=128)), 16)
                    Lq.inc(sy.dma_start(out=btmp[21:22, 0:42], in_=b1_in[layer, 5418:5460].unsqueeze(0)), 16)
                YB(_bl2)
                bl2_l = Lq.n

                def _bt2(t):
                    t.wait_ge(Lq.h, bl2_l)
                    PE_.inc(t.transpose(pfx[:, 0:22], btmp[0:22, :], idf[0:22, 0:22]))
                TB(_bt2)

                def _btv2(v):
                    v.wait_ge(PE_.h, PE_.n)
                    VE.inc(v.tensor_copy(b1gT[:, 0:22], pfx[:, 0:22]))
                VB(_btv2)

                def _bl3(sy):
                    sy.wait_ge(VE.h, VE.n)
                    Lq.inc(sy.dma_start(out=b2row[:, :], in_=lg_in[layer:layer + 1, :]), 16)
                YB(_bl3)
                bl3_l = Lq.n

                def _bt3(t):
                    t.wait_ge(Lq.h, bl3_l)
                    PE_.inc(t.matmul(pw[0][:, :], onesf[0:1, :], b2row[0:1, 0:512],
                                     start=True, stop=True))
                    PE_.inc(t.matmul(pw[1][:, :], onesf[0:1, :], b2row[0:1, 512:1024],
                                     start=True, stop=True))
                TB(_bt3)

                def _btv3(v):
                    v.wait_ge(PE_.h, PE_.n)
                    VE.inc(v.tensor_copy(lg_rep[:, 0:512], pw[0][:, :]))
                    VE.inc(v.tensor_copy(lg_rep[:, 512:1024], pw[1][:, :]))
                VB(_btv3)

                def _bl4(sy):
                    sy.wait_ge(VE.h, VE.n)
                    Lq.inc(sy.dma_start(out=b2row[:, :], in_=lb_in[layer:layer + 1, :]), 16)
                YB(_bl4)
                bl4_l = Lq.n

                def _bt4(t):
                    t.wait_ge(Lq.h, bl4_l)
                    PE_.inc(t.matmul(pw[0][:, :], onesf[0:1, :], b2row[0:1, 0:512],
                                     start=True, stop=True))
                    PE_.inc(t.matmul(pw[1][:, :], onesf[0:1, :], b2row[0:1, 512:1024],
                                     start=True, stop=True))
                TB(_bt4)

                def _btv4(v):
                    v.wait_ge(PE_.h, PE_.n)
                    VE.inc(v.tensor_copy(lb_rep[:, 0:512], pw[0][:, :]))
                    VE.inc(v.tensor_copy(lb_rep[:, 512:1024], pw[1][:, :]))
                VB(_btv4)

                def _bl5(sy):
                    sy.wait_ge(VE.h, VE.n)
                    Lq.inc(sy.dma_start(out=b2row[:, :], in_=b2_in[layer:layer + 1, :]), 16)
                YB(_bl5)
                bl5_l = Lq.n

                def _bt5(t):
                    t.wait_ge(Lq.h, bl5_l)
                    PE_.inc(t.matmul(pw[0][:, :], onesf[0:1, :], b2row[0:1, 0:512],
                                     start=True, stop=True))
                    PE_.inc(t.matmul(pw[1][:, :], onesf[0:1, :], b2row[0:1, 512:1024],
                                     start=True, stop=True))
                TB(_bt5)

                def _btv5(v):
                    v.wait_ge(PE_.h, PE_.n)
                    VE.inc(v.tensor_copy(b2rep[:, 0:512], pw[0][:, :]))
                    VE.inc(v.tensor_copy(b2rep[:, 512:1024], pw[1][:, :]))
                VB(_btv5)
                init_ve = VE.n

                mu4 = EG(nc.sbuf_tensor(f"g2{layer}_mu4", [128, 4], F32))
                tm4 = EG(nc.sbuf_tensor(f"g2{layer}_tm4", [128, 4], F32))
                sa4 = EG(nc.sbuf_tensor(f"g2{layer}_sa4", [128, 4], F32))
                sb4 = EG(nc.sbuf_tensor(f"g2{layer}_sb4", [128, 4], F32))
                ve4 = EG(nc.sbuf_tensor(f"g2{layer}_ve4", [128, 4], F32))
                rstd4 = EG(nc.sbuf_tensor(f"g2{layer}_rstd4", [128, 4], F32))
                negb4 = EG(nc.sbuf_tensor(f"g2{layer}_negb4", [128, 4], F32))

                ptb_free = [init_ve]
                hid_pe = {}
                wres_ve = {}
                hev_se = []
                w2_used = {}
                for cc in range(8):
                    stat_se0 = SE_.n
                    for tt in range(4):
                        r = 1 + cc * WIN + tt * 128
                        ybi = tt % 2

                        def _yl(sy):
                            sy.wait_ge(VE.h, VE.n)
                            sy.wait_ge(SE_.h, SE_.n)
                            Yq.inc(sy.dma_start(out=y1f[ybi][:, :], in_=ybin[r:r + 128, :]), 16)
                            Yq.inc(sy.dma_start(out=y1s[ybi][:, :], in_=ybin[r - 1:r + 127, 512:1024]), 16)
                        YB(_yl)
                        yl_l = Yq.n

                        def _st1(v):
                            v.wait_ge(Yq.h, yl_l)
                            VE.inc(v.reduce_sum(mu4[:, tt:tt + 1], y1f[ybi][:, 0:512], axis=AX.X))
                            VE.inc(v.reduce_sum(tm4[:, tt:tt + 1], y1s[ybi][:, :], axis=AX.X))
                        VB(_st1)

                        def _st2(s):
                            s.wait_ge(Yq.h, yl_l)
                            s.wait_ge(SE_.h, SE_.n)
                            SE_.inc(s.activation(sqd[:, 0:512], y1f[ybi][:, 0:512], AF.Square,
                                                 accum_out=sa4[:, tt:tt + 1]))
                            SE_.inc(s.activation(sqd[:, 512:1024], y1s[ybi][:, :], AF.Square,
                                                 accum_out=sb4[:, tt:tt + 1]))
                        SB(_st2)

                    def _chain(v):
                        v.wait_ge(VE.h, VE.n)
                        v.wait_ge(SE_.h, SE_.n)
                        VE.inc(v.tensor_add(mu4[:, :], mu4[:, :], tm4[:, :]))
                        VE.inc(v.tensor_add(sa4[:, :], sa4[:, :], sb4[:, :]))
                        v.wait_ge(VE.h, VE.n)
                        VE.inc(v.tensor_scalar(mu4[:, :], mu4[:, :], 1.0 / D, None, ALU.mult))
                        VE.inc(v.tensor_scalar(ve4[:, :], sa4[:, :], 1.0 / D, EPS, ALU.mult, ALU.add))
                        v.wait_ge(VE.h, VE.n)
                        VE.inc(v.scalar_tensor_tensor(tm4[:, :], mu4[:, :], -1.0, mu4[:, :],
                                                      ALU.mult, ALU.mult))
                        v.wait_ge(VE.h, VE.n)
                        VE.inc(v.tensor_add(ve4[:, :], ve4[:, :], tm4[:, :]))
                    VB(_chain)

                    def _chain2v(v):
                        v.wait_ge(VE.h, VE.n)
                        VE.inc(v.reciprocal(rstd4[:, :], ve4[:, :]))
                    VB(_chain2v)

                    def _chain2(s):
                        s.wait_ge(VE.h, VE.n)
                        SE_.inc(s.activation(rstd4[:, :], rstd4[:, :], AF.Sqrt))
                    SB(_chain2)

                    def _chain3(v):
                        v.wait_ge(SE_.h, SE_.n)
                        VE.inc(v.scalar_tensor_tensor(negb4[:, :], mu4[:, :], -1.0, rstd4[:, :],
                                                      ALU.mult, ALU.mult))
                    VB(_chain3)
                    chain_ve = VE.n

                    for tt in range(4):
                        r = 1 + cc * WIN + tt * 128
                        ybi = tt % 2

                        def _yl2(sy):
                            sy.wait_ge(VE.h, VE.n)
                            sy.wait_ge(SE_.h, SE_.n)
                            Yq.inc(sy.dma_start(out=y1f[ybi][:, :], in_=ybin[r:r + 128, :]), 16)
                            Yq.inc(sy.dma_start(out=y1s[ybi][:, :], in_=ybin[r - 1:r + 127, 512:1024]), 16)
                        YB(_yl2)
                        yl2_l = Yq.n

                        def _nrm(s):
                            s.wait_ge(Yq.h, yl2_l)
                            s.wait_ge(VE.h, VE.n)
                            s.wait_ge(PE_.h, PE_.n)
                            s.wait_ge(VE.h, chain_ve)
                            SE_.inc(s.activation(hnb[:, 0:512], y1f[ybi][:, 0:512], AF.Identity,
                                                 bias=negb4[:, tt:tt + 1], scale=rstd4[:, tt:tt + 1]))
                            SE_.inc(s.activation(hnb[:, 512:1024], y1s[ybi][:, :], AF.Identity,
                                                 bias=negb4[:, tt:tt + 1], scale=rstd4[:, tt:tt + 1]))
                        SB(_nrm)
                        ln_se = SE_.n

                        def _aff(v):
                            v.wait_ge(SE_.h, ln_se)
                            VE.inc(v.tensor_mul(hnb[:, :], hnb[:, :], lg_rep[:, :]))
                            v.wait_ge(VE.h, VE.n)
                            VE.inc(v.tensor_add(hnb[:, :], hnb[:, :], lb_rep[:, :]))
                        VB(_aff)
                        aff_ve = VE.n

                        for g4 in range(2):
                            def _t1(t):
                                t.wait_ge(VE.h, aff_ve)
                                t.wait_ge(VE.h, ptb_free[0])
                                if cc - 1 in hid_pe and g4 == 0 and tt == 0:
                                    t.wait_ge(PE_.h, hid_pe[cc - 1])
                                for j in range(4):
                                    dk = g4 * 4 + j
                                    PE_.inc(t.transpose(ptb[:, j * 128:(j + 1) * 128],
                                                        hnb[:, dk * 128:(dk + 1) * 128], idb[:]))
                            TB(_t1)
                            t1_pe = PE_.n

                            def _e1(v):
                                v.wait_ge(PE_.h, t1_pe)
                                if cc - 1 in hid_pe and g4 == 0 and tt == 0:
                                    v.wait_ge(PE_.h, hid_pe[cc - 1])
                                for j in range(4):
                                    dk = g4 * 4 + j
                                    VE.inc(v.tensor_copy(hlnT[dk][:, tt * 128:(tt + 1) * 128],
                                                         ptb[:, j * 128:(j + 1) * 128]))
                                ptb_free[0] = VE.n
                            VB(_e1)

                    hln_ve = VE.n

                    for htx in range(NHT):
                        nP = HT_SIZES[htx]
                        a0 = HT_OFF[htx]
                        g0 = HID + HT_OFF[htx]
                        hpar = htx % 2

                        def _hmm(t):
                            if htx == 0:
                                t.wait_ge(Gq.h, g2_w_gq)
                                t.wait_ge(VE.h, hln_ve)
                            if len(hev_se) >= 2:
                                t.wait_ge(SE_.h, hev_se[-2])
                            for kt in range(DT):
                                PE_.inc(t.matmul(pa[hpar][0:nP, :], w1_sb[kt][:, a0:a0 + nP],
                                                 hlnT[kt][:, :], start=(kt == 0), stop=(kt == DT - 1)))
                            for kt in range(DT):
                                PE_.inc(t.matmul(pg[hpar][0:nP, :], w1_sb[kt][:, g0:g0 + nP],
                                                 hlnT[kt][:, :], start=(kt == 0), stop=(kt == DT - 1)))
                        TB(_hmm)
                        hmm_pe = PE_.n

                        def _hev(s):
                            s.wait_ge(PE_.h, hmm_pe)
                            if len(hev_se) >= 2:
                                s.wait_ge(VE.h, hmu_ve[-2])
                            SE_.inc(s.activation(asb[hpar][0:nP, :], pa[hpar][0:nP, :], AF.Identity,
                                                 bias=b1aT[0:nP, htx:htx + 1]))
                            SE_.inc(s.activation(gsb[hpar][0:nP, :], pg[hpar][0:nP, :], AF.Gelu,
                                                 bias=b1gT[0:nP, htx:htx + 1]))
                            hev_se.append(SE_.n)
                        SB(_hev)

                        def _hmu(v):
                            v.wait_ge(SE_.h, hev_se[-1])
                            if cc > 0:
                                v.wait_ge(PE_.h, w2_used[htx])
                            VE.inc(v.tensor_mul(gt[htx][0:nP, :], asb[hpar][0:nP, :],
                                                gsb[hpar][0:nP, :]))
                            hmu_ve.append(VE.n)
                        if cc == 0 and htx == 0:
                            hmu_ve = []
                        VB(_hmu)

                    gt_ve = VE.n
                    w2_used = {}

                    for tt in range(4):
                        ybi = tt % 2
                        rr = 1 + cc * WIN + tt * 128

                        def _ylr(sy):
                            sy.wait_ge(VE.h, VE.n)
                            sy.wait_ge(SE_.h, SE_.n)
                            Yq.inc(sy.dma_start(out=y1f[ybi][:, :], in_=ybin[rr:rr + 128, :]), 16)
                        YB(_ylr)
                        ylr_l = Yq.n

                        def _wmm(t):
                            if tt == 0:
                                t.wait_ge(VE.h, gt_ve)
                            else:
                                t.wait_ge(VE.h, VE.n)
                            for htx in range(NHT):
                                nP = HT_SIZES[htx]
                                for nb in range(2):
                                    PE_.inc(t.matmul(pw[nb][:, :],
                                                     gt[htx][0:nP, tt * 128:(tt + 1) * 128],
                                                     w2_sb[htx][0:nP, nb * 512:(nb + 1) * 512],
                                                     start=(htx == 0), stop=(htx == NHT - 1)))
                                w2_used[htx] = PE_.n
                        TB(_wmm)
                        wmm_pe = PE_.n

                        def _wres(v):
                            v.wait_ge(PE_.h, wmm_pe)
                            v.wait_ge(Yq.h, ylr_l)
                            v.wait_ge(Sq.h, Sq.n)
                            VE.inc(v.tensor_add(y2t[ybi][:, 0:512], y1f[ybi][:, 0:512], pw[0][:, :]))
                            VE.inc(v.tensor_add(y2t[ybi][:, 512:1024], y1f[ybi][:, 512:1024],
                                                pw[1][:, :]))
                            v.wait_ge(VE.h, VE.n)
                            VE.inc(v.tensor_add(y2t[ybi][:, :], y2t[ybi][:, :], b2rep[:, :]))
                        VB(_wres)

                        def _wst(sy):
                            sy.wait_ge(VE.h, VE.n)
                            ro = out_base + cc * WIN + tt * 128
                            Sq.inc(sy.dma_start(out=ybout[ro:ro + 128, :], in_=y2t[ybi][:, :]), 16)
                        YB(_wst)
                    hid_pe[cc] = PE_.n
                    wres_ve[cc] = VE.n

                g2_stores = Sq.n

                if tail_phase is not None:
                    xch_phase(tail_phase,
                              [(lambda s: tail_xch[s, :, :], ybout[out_base + T - 513:out_base + T, :])],
                              [(ybout[0:513, :], lambda s: tail_xch[s, :, :])],
                              wait_sem=Sq, wait_val=g2_stores)
                    if not comm:
                        def _zf2(v):
                            v.wait_ge(Sq.h, Sq.n)
                            VE.inc(v.memset(halot[:, :], 0.0))
                        VB(_zf2)

                        def _zs2(sy):
                            sy.wait_ge(VE.h, VE.n)
                            for r0, nr in [(0, 128), (128, 128), (256, 128), (384, 128), (512, 1)]:
                                Sq.inc(sy.dma_start(out=ybout[r0:r0 + nr, :], in_=halot[:nr, :]), 16)
                        YB(_zs2)
                    halo_tiles = [(0, 128), (128, 128), (256, 128), (384, 128), (512, 1)]
                    for r0, nr in halo_tiles:
                        def _hld(sy):
                            sy.wait_ge(XR.h, XR.n)
                            sy.wait_ge(VE.h, VE.n)
                            sy.wait_ge(Sq.h, Sq.n)
                            Yq.inc(sy.dma_start(out=halot[:nr, :], in_=ybout[r0:r0 + nr, :]), 16)
                        YB(_hld)
                        hl = Yq.n

                        def _hsc(v):
                            v.wait_ge(Yq.h, hl)
                            VE.inc(v.tensor_scalar(halot[:nr, :], halot[:nr, :], shsc[:nr, :],
                                                   None, ALU.mult))
                        VB(_hsc)

                        def _hst(sy):
                            sy.wait_ge(VE.h, VE.n)
                            Sq.inc(sy.dma_start(out=ybout[r0:r0 + nr, :], in_=halot[:nr, :]), 16)
                        YB(_hst)

            stage_barrier()

        if stages >= 2:
            run_g1(0, y0buf, y1buf, xch_y1, 2)
        if stages >= 3:
            run_g2(0, y1buf, y2buf, 3, xch_y2, 513)
        if stages >= 4:
            run_g1(1, y2buf, y3buf, xch_y3, 4)
        if stages >= 5:
            run_g2(1, y3buf, y4buf, None, None, 0)

        if stages >= 6:
          with contextlib.ExitStack() as sf:
            EF = sf.enter_context
            y4g = [EF(nc.sbuf_tensor(f"f_y4{i}", [128, D], F32)) for i in range(4)]
            xmg = [EF(nc.sbuf_tensor(f"f_xm{i}", [128, D], F32)) for i in range(4)]
            og = [EF(nc.sbuf_tensor(f"f_ot{i}", [128, D], F32)) for i in range(4)]
            muF = EF(nc.sbuf_tensor("f_mu", [128, 4], F32))
            saF = EF(nc.sbuf_tensor("f_sa", [128, 4], F32))
            veF = EF(nc.sbuf_tensor("f_ve", [128, 4], F32))
            tmF = EF(nc.sbuf_tensor("f_tm", [128, 4], F32))
            rsF = EF(nc.sbuf_tensor("f_rs", [128, 4], F32))
            nbF = EF(nc.sbuf_tensor("f_nb", [128, 4], F32))

            for grp in range(NT // 4):
                g0 = grp * 4
                sqb = Sq.n

                def _fl(sy):
                    sy.wait_ge(VE.h, VE.n)
                    sy.wait_ge(SE_.h, SE_.n)
                    for t in range(4):
                        r0 = (g0 + t) * 128
                        Yq.inc(sy.dma_start(out=y4g[t][:, :], in_=y4buf[r0:r0 + 128, :]), 16)
                        Yq.inc(sy.dma_start(out=xmg[t][:, :], in_=xmidbuf[r0:r0 + 128, :]), 16)
                YB(_fl)
                fl_l = Yq.n

                def _p1v(v):
                    v.wait_ge(Yq.h, fl_l)
                    v.wait_ge(VE.h, VE.n)
                    for t in range(4):
                        VE.inc(v.reduce_sum(muF[:, t:t + 1], y4g[t][:, :], axis=AX.X))
                VB(_p1v)

                def _p1s(s):
                    s.wait_ge(Yq.h, fl_l)
                    for t in range(4):
                        s.wait_ge(SE_.h, SE_.n)
                        SE_.inc(s.activation(sqd[:, :], y4g[t][:, :], AF.Square,
                                             accum_out=saF[:, t:t + 1]))
                SB(_p1s)

                def _ch(v):
                    v.wait_ge(VE.h, VE.n)
                    v.wait_ge(SE_.h, SE_.n)
                    VE.inc(v.tensor_scalar(muF[:, :], muF[:, :], 1.0 / D, None, ALU.mult))
                    VE.inc(v.tensor_scalar(veF[:, :], saF[:, :], 1.0 / D, EPS, ALU.mult, ALU.add))
                    v.wait_ge(VE.h, VE.n)
                    VE.inc(v.scalar_tensor_tensor(tmF[:, :], muF[:, :], -1.0, muF[:, :],
                                                  ALU.mult, ALU.mult))
                    v.wait_ge(VE.h, VE.n)
                    VE.inc(v.tensor_add(veF[:, :], veF[:, :], tmF[:, :]))
                    v.wait_ge(VE.h, VE.n)
                    VE.inc(v.reciprocal(rsF[:, :], veF[:, :]))
                VB(_ch)

                def _chs(s):
                    s.wait_ge(VE.h, VE.n)
                    SE_.inc(s.activation(rsF[:, :], rsF[:, :], AF.Sqrt))
                SB(_chs)

                def _ch3(v):
                    v.wait_ge(SE_.h, SE_.n)
                    VE.inc(v.scalar_tensor_tensor(nbF[:, :], muF[:, :], -1.0, rsF[:, :],
                                                  ALU.mult, ALU.mult))
                VB(_ch3)
                chF = VE.n

                for t in range(4):
                    r0 = (g0 + t) * 128

                    def _nrm(s):
                        s.wait_ge(VE.h, chF)
                        s.wait_ge(Sq.h, sqb)
                        SE_.inc(s.activation(og[t][:, :], y4g[t][:, :], AF.Identity,
                                             bias=nbF[:, t:t + 1], scale=rsF[:, t:t + 1]))
                    SB(_nrm)
                    n_se = SE_.n

                    def _fa(v):
                        v.wait_ge(SE_.h, n_se)
                        VE.inc(v.tensor_add(og[t][:, :], og[t][:, :], xmg[t][:, :]))
                    VB(_fa)
                    a_ve = VE.n

                    def _fs(sy):
                        sy.wait_ge(VE.h, a_ve)
                        Sq.inc(sy.dma_start(out=out_my[r0:r0 + 128, :], in_=og[t][:, :]), 16)
                    YB(_fs)

        if debug:
            REQ = {"dbg_xmid": 1, "dbg_y0": 1, "dbg_y1": 2, "dbg_y2": 3,
                   "dbg_y3": 4, "dbg_y4": 5}

            def _dbg(sy):
                sy.wait_ge(Sq.h, Sq.n)
                for nm, (oap, buf) in dbg.items():
                    if stages >= REQ[nm]:
                        Sq.inc(sy.dma_start(out=oap, in_=buf[:, :]), 16)
            YB(_dbg)

        def _fin(sy):
            sy.wait_ge(Sq.h, Sq.n)
            if Gq.n:
                sy.wait_ge(Gq.h, Gq.n)
            if XR.n:
                sy.wait_ge(XR.h, XR.n)
        YB(_fin)

    return nc


def _host_tables(h):
    t0 = h * T - WIN
    ts = np.arange(t0, t0 + TBL_ROWS, dtype=np.float64)
    half = np.arange(0, DH, 2, dtype=np.float64)
    inv_freq = 1.0 / 10000.0 ** (half / DH)
    f = ts[:, None] * inv_freq[None, :]
    freqs = np.concatenate([f, f], axis=1)
    base = (half + 0.4 * DH) / (1.4 * DH)
    power = (ts - N // 2) / float(WIN // 2)
    sc = base[None, :] ** power[:, None]
    scale = np.concatenate([sc, sc], axis=1)
    cos, sin = np.cos(freqs), np.sin(freqs)
    tqc = (cos * scale).astype(np.float32)
    tqs = (sin * scale).astype(np.float32)
    tkc = (cos / scale).astype(np.float32)
    tks = (sin / scale).astype(np.float32)
    if t0 < 0:
        nz = -t0
        for a in (tqc, tqs, tkc, tks):
            a[:nz] = 0.0
    def perm(a):
        return np.ascontiguousarray(
            a.reshape(TBL_T, 128, DH).transpose(1, 0, 2).reshape(128, TBL_T * DH))
    return perm(tqc), perm(tqs), perm(tkc), perm(tks)


def _host_masks(h):
    p = np.arange(128)[:, None]
    col = np.arange(1408)[None, :]
    d = col - 384
    big = np.where((d - p >= 0) & (d - p <= WIN), 0.0, NEG).astype(np.float32)
    m0 = np.zeros((8, 128, WIN), np.float32)
    q = np.arange(WIN)[None, :]
    for kt in range(8):
        if h == 0:
            if kt < 4:
                m0[kt, :, :] = NEG
            else:
                kg = (kt - 4) * 128 + np.arange(128)[:, None]
                m0[kt] = np.where(q - kg >= 0, 0.0, NEG)
        else:
            dd = q + 512 - kt * 128 - np.arange(128)[:, None]
            m0[kt] = np.where((dd >= 0) & (dd <= WIN), 0.0, NEG)
    return big.astype(ml_dtypes.bfloat16), m0.astype(ml_dtypes.bfloat16)


_NC_CACHE = {}


def _get_nc(debug=False, stages=99, comm=True, race=True):
    key = (debug, stages, comm, race)
    if key not in _NC_CACHE:
        _NC_CACHE[key] = build_nc(debug, stages, comm, race)
    return _NC_CACHE[key]


def make_in_maps(inputs, rng=None):
    x = np.ascontiguousarray(np.asarray(inputs["L0_attn_x"], dtype=np.float32))
    shared = {
        "wqkv": np.ascontiguousarray(np.asarray(inputs["L0_ffn_wqkv"], np.float32)),
        "wo": np.ascontiguousarray(np.asarray(inputs["L0_ffn_wo"], np.float32)),
        "w1": np.ascontiguousarray(np.asarray(inputs["L0_ffn_w1"], np.float32)),
        "w2": np.ascontiguousarray(np.asarray(inputs["L0_ffn_w2"], np.float32)),
        "qscale": np.ascontiguousarray(np.asarray(inputs["L0_ffn_q_scale"], np.float32)),
        "kscale": np.ascontiguousarray(np.asarray(inputs["L0_ffn_k_scale"], np.float32)),
        "ln_g": np.ascontiguousarray(np.asarray(inputs["L0_ffn_ln_g"], np.float32)),
        "ln_b": np.ascontiguousarray(np.asarray(inputs["L0_ffn_ln_b"], np.float32)),
        "b1": np.ascontiguousarray(np.asarray(inputs["L0_ffn_b1"], np.float32)),
        "b2": np.ascontiguousarray(np.asarray(inputs["L0_ffn_b2"], np.float32)),
        "idb": np.eye(128).astype(ml_dtypes.bfloat16),
        "idf": np.eye(128).astype(np.float32),
        "onesf": np.ones((1, 128), np.float32),
    }
    if rng is None:
        rng = np.random.default_rng()
    nonce = np.array([[int(rng.integers(1, 2**31 - 1))]], np.int32)
    in_maps = []
    for c in range(8):
        b, h = c // 2, c % 2
        tqc, tqs, tkc, tks = _host_tables(h)
        big, m0 = _host_masks(h)
        im = dict(shared)
        im["x_my"] = np.ascontiguousarray(x[b, h * T:(h + 1) * T])
        im["x_oth"] = np.ascontiguousarray(x[b, (1 - h) * T:(2 - h) * T])
        im["tqc"], im["tqs"], im["tkc"], im["tks"] = tqc, tqs, tkc, tks
        im["bigmask"], im["maskw0"] = big, m0
        im["nonce"] = nonce
        im["hscale"] = np.full((128, 1), float(h), np.float32)
        in_maps.append(im)
    return in_maps


def kernel(**inputs):
    nc = _get_nc(debug=False)
    in_maps = make_in_maps(inputs)
    res = run_bass_kernel_spmd(nc, in_maps, core_ids=list(range(8)))
    out = np.empty((B, N, D), np.float32)
    for c in range(8):
        b, h = c // 2, c % 2
        out[b, h * T:(h + 1) * T] = res.results[c]["out_my"]
    return out
